# version 10
# speedup vs baseline: 1.4617x; 1.0052x over previous
import numpy as np
from contextlib import ExitStack

import concourse.bass as bass
import concourse.mybir as mybir
from concourse.bass_utils import run_bass_kernel_spmd
from concourse.bass_core_rust import IndirectOffsetOnAxis

F32 = mybir.dt.float32
BF16 = mybir.dt.bfloat16
I32 = mybir.dt.int32
ALU = mybir.AluOpType
AX = mybir.AxisListType
AF = mybir.ActivationFunctionType

B, S, D = 16, 2048, 1024
H, HD = 16, 64
NSEQ = 2
NT = S // 128
NG = S // 512
NCH = D // 128
FF = 2048
NFC = FF // 128
NEXP, NAV = 8, 128
INV_SCALE_A = 1.0 / 32.0
SCALE_B = float(64 ** -0.25)
EPS_LN = 1e-5
EPS_B = 1e-6
TWO_PI_HI = float(np.float32(2 * np.pi))
TWO_PI_LO = float(2 * np.pi - np.float64(np.float32(2 * np.pi)))
LN_THETA_SC = float(-2.0 * np.log(10000.0) / 64.0)

DEBUG_AA = False
DEBUG_FF = False
SKIP_AB = set()
EXTRA_BARRIERS = set()
DBG_STATE = {"dumped": False}

ENGINES = ("tensor", "vector", "scalar", "gpsimd", "sync")
ESEM = {"tensor": "pe", "vector": "dv", "scalar": "ac", "gpsimd": "gp", "sync": "sp"}
QSEM = {"gpsimd": "qg", "sync": "qs", "scalar": "qa"}


class Prog:
    def __init__(self, nc, es):
        self.nc = nc
        self.es = es
        self.streams = {e: [] for e in ENGINES}
        self.semh = {}
        self.semv = {}
        self.waited = {e: {} for e in ENGINES}
        self.last_cell = {e: None for e in ENGINES}
        self.last_is_dma = {e: False for e in ENGINES}
        self.last_tok = {e: None for e in ENGINES}
        self.active = {e: False for e in ENGINES}
        self.dma_rr = {e: 0 for e in ENGINES}
        self.uid = 0

    def mksem(self, name):
        self.semh[name] = self.es.enter_context(self.nc.semaphore(name))
        self.semv[name] = 0

    def tok(self, name):
        return (name, self.semv[name])

    def wait(self, eng, tok):
        if tok is None:
            return
        name, val = tok
        if val <= 0 or self.waited[eng].get(name, 0) >= val:
            return
        self.waited[eng][name] = val
        h = self.semh[name]
        self.streams[eng].append(lambda e, h=h, v=val: e.wait_ge(h, v))

    def run(self, eng, fn, inc=None, amt=1, _dma=False):
        if eng in ("vector", "scalar", "gpsimd") and not _dma:
            name = ESEM[eng]
            prev = self.semv[name]
            if prev > 0:
                self.wait(eng, (name, prev))
            inc = name
            amt = 1
        tok = None
        h = None
        if inc is not None:
            self.semv[inc] += amt
            tok = (inc, self.semv[inc])
            h = self.semh[inc]
        cell = [None]
        self.last_cell[eng] = cell
        self.last_is_dma[eng] = _dma
        self.active[eng] = True

        self.last_tok[eng] = tok

        def thunk(e, fn=fn, h=h, amt=amt, cell=cell):
            ins = fn(e)
            cell[0] = ins
            if h is not None:
                ins.then_inc(h, amt)
        self.streams[eng].append(thunk)
        return tok

    DMA_POOL = {"gpsimd": 8, "sync": 6, "scalar": 2}

    def dma(self, eng, fn):
        q = QSEM[eng]
        k = self.DMA_POOL[eng]
        j = self.dma_rr[eng] % k
        self.dma_rr[eng] += 1
        sem = f"{q}{j}"
        self.wait(eng, (sem, self.semv[sem]))
        return self.run(eng, fn, inc=sem, amt=16, _dma=True)

    def retro(self, eng):
        if self.last_tok[eng] is not None:
            return self.last_tok[eng]
        name = ESEM[eng]
        cell = self.last_cell[eng]
        assert cell is not None, f"retro on {eng} with no instructions yet"
        amt = 16 if self.last_is_dma[eng] else 1
        self.semv[name] += amt
        tok = (name, self.semv[name])
        h = self.semh[name]

        def thunk(e, cell=cell, h=h, amt=amt):
            cell[0].then_inc(h, amt)
        self.streams[eng].append(thunk)
        self.last_tok[eng] = tok
        return tok

    def barrier(self):
        toks = []
        for eng in ENGINES:
            if not self.active[eng]:
                continue
            q = QSEM.get(eng)
            if q is not None:
                for j in range(self.DMA_POOL[eng]):
                    sem = f"{q}{j}"
                    if self.semv[sem] > 0:
                        self.wait(eng, self.tok(sem))
            toks.append(self.retro(eng))
        for eng in ENGINES:
            for tk in toks:
                self.wait(eng, tk)
            self.active[eng] = False


def build_program(phases=("attn_a", "ffn", "attn_b", "ffn"), nseq=NSEQ):
    nc = bass.Bass("TRN2", target_bir_lowering=False)

    xin = nc.dram_tensor("xin", [nseq, D, S], F32, kind="ExternalInput")
    yout = nc.dram_tensor("yout", [nseq, D, S], F32, kind="ExternalOutput")
    if DEBUG_FF:
        dbgf = {
            "f_lg": nc.dram_tensor("f_lg", [1, NEXP], F32, kind="ExternalOutput"),
            "f_sel": nc.dram_tensor("f_sel", [1, 1], F32, kind="ExternalOutput"),
            "f_xl": nc.dram_tensor("f_xl", [128, NCH], F32, kind="ExternalOutput"),
            "f_g": nc.dram_tensor("f_g", [1, NAV], F32, kind="ExternalOutput"),
            "f_w1": nc.dram_tensor("f_w1", [128, NCH, FF], BF16, kind="ExternalOutput"),
            "f_w2": nc.dram_tensor("f_w2", [128, NFC, D], BF16, kind="ExternalOutput"),
            "f_b1": nc.dram_tensor("f_b1", [128, NFC], F32, kind="ExternalOutput"),
            "f_h1": nc.dram_tensor("f_h1", [128, NFC, 512], BF16, kind="ExternalOutput"),
            "f_rs": nc.dram_tensor("f_rs", [128, NCH, 512], BF16, kind="ExternalOutput"),
            "f_idx": nc.dram_tensor("f_idx", [128, NCH], I32, kind="ExternalOutput"),
        }
    if DEBUG_AA:
        dbg = {
            "d_k": nc.dram_tensor("d_k", [128, NCH, S], BF16, kind="ExternalOutput"),
            "d_v": nc.dram_tensor("d_v", [128, NT, D], BF16, kind="ExternalOutput"),
            "d_q": nc.dram_tensor("d_q", [128, NCH, 512], BF16, kind="ExternalOutput"),
            "d_afm": nc.dram_tensor("d_afm", [128, NCH, 512], BF16, kind="ExternalOutput"),
            "d_e": nc.dram_tensor("d_e", [128, 2, 1024], BF16, kind="ExternalOutput"),
            "d_x": nc.dram_tensor("d_x", [128, NCH, S], F32, kind="ExternalOutput"),
            "d_rstd": nc.dram_tensor("d_rstd", [128, S], BF16, kind="ExternalOutput"),
            "d_mr": nc.dram_tensor("d_mr", [128, S], BF16, kind="ExternalOutput"),
            "d_xn": nc.dram_tensor("d_xn", [128, NCH, 512], BF16, kind="ExternalOutput"),
            "d_wa": nc.dram_tensor("d_wa", [128, NCH, D], BF16, kind="ExternalOutput"),
            "d_kraw": nc.dram_tensor("d_kraw", [128, D], BF16, kind="ExternalOutput"),
            "d_krop": nc.dram_tensor("d_krop", [128, D], BF16, kind="ExternalOutput"),
            "d_mrow": nc.dram_tensor("d_mrow", [1, S], F32, kind="ExternalOutput"),
            "d_rrow": nc.dram_tensor("d_rrow", [1, S], F32, kind="ExternalOutput"),
            "d_vrow": nc.dram_tensor("d_vrow", [1, S], F32, kind="ExternalOutput"),
            "d_c2t": nc.dram_tensor("d_c2t", [128, NT, 64], BF16, kind="ExternalOutput"),
            "d_s2t": nc.dram_tensor("d_s2t", [128, NT, 64], BF16, kind="ExternalOutput"),
            "d_ang": nc.dram_tensor("d_ang", [128, NT, 32], F32, kind="ExternalOutput"),
            "d_pang": nc.dram_tensor("d_pang", [128, NT, 32], F32, kind="ExternalOutput"),
        }
    dr = {}
    for name, shape in [
        ("aWqT", [D, D]), ("aWkT", [D, D]), ("aWvT", [D, D]), ("aWoT", [D, D]),
        ("abq", [1, D]), ("abk", [1, D]), ("abv", [1, D]),
        ("bWqT", [D, D]), ("bWkT", [D, D]), ("bWvT", [D, D]), ("bWoT", [D, D]),
        ("bbo_fm", [128, NCH]),
        ("fWp1", [D, NAV]), ("fbp1", [1, NAV]), ("fWp2", [NAV, NEXP]),
        ("fbp2", [1, NEXP]),
        ("fW1", [NEXP * D, FF]), ("fW2", [NEXP * FF, D]),
        ("fb1r", [NEXP * 128, NFC]), ("fb2r", [NEXP * 128, NCH]),
        ("fWo", [D, D]), ("fbo_fm", [128, NCH]),
    ]:
        dr[name] = nc.dram_tensor(name, shape, F32, kind="ExternalInput")

    es = ExitStack()
    p = Prog(nc, es)
    for s in ("pe", "dv", "ac", "gp", "sp"):
        p.mksem(s)
    for q, eng in (("qg", "gpsimd"), ("qs", "sync"), ("qa", "scalar")):
        for j in range(Prog.DMA_POOL[eng]):
            p.mksem(f"{q}{j}")
    r = p.run

    def sb(name, shape, dt=F32, stk=es):
        p.uid += 1
        return stk.enter_context(nc.sbuf_tensor(f"{name}_{p.uid}", shape, dt))

    def psum_(name, shape, dt=F32, stk=es):
        p.uid += 1
        return stk.enter_context(nc.psum_tensor(f"{name}_{p.uid}", shape, dt))

    x_fm = sb("x_fm", [128, NCH, S])
    ident_b = sb("ident_b", [128, 128], BF16)
    ones_row = sb("ones_row", [1, 128])
    ones_row_b = sb("ones_row_b", [1, 128], BF16)
    ones_col = sb("ones_col", [128, 1])
    ones_col_b = sb("ones_col_b", [128, 1], BF16)
    ones_64 = sb("ones_64", [1, 64])
    eps_sb = sb("eps_sb", [1, 1])
    iota_p = sb("iota_p", [128, 1])
    c2t = sb("c2t", [128, NT, 64], BF16)
    s2t = sb("s2t", [128, NT, 64], BF16)
    rev8 = sb("rev8", [1, NEXP])
    rstd_b = sb("rstd_b", [128, S], BF16)
    mrstd_b = sb("mrstd_b", [128, S], BF16)
    rstd1_b = [None]
    m_last = sb("m_last", [1, 1])
    r_last = sb("r_last", [1, 1])
    sq_scr = sb("sq_scr", [128, 512])
    xn_sb = sb("xn_sb", [128, NCH, 512], BF16)
    pos_row = None
    freqs = None
    init_stk = ExitStack()

    def emit_init():
        nonlocal pos_row, freqs
        pos_row = sb("pos_row", [1, S], F32, init_stk)
        freqs = sb("freqs", [1, 32], F32, init_stk)
        r("gpsimd", lambda g: g.memset(ident_b[:], 0.0))
        r("gpsimd", lambda g: g.affine_select(
            out=ident_b[:], in_=ident_b[:], compare_op=ALU.not_equal,
            fill=1.0, base=0, pattern=[[-1, 128]], channel_multiplier=1))
        for tile, val in ((ones_row, 1.0), (ones_row_b, 1.0), (ones_col, 1.0),
                          (ones_col_b, 1.0), (ones_64, 1.0), (eps_sb, EPS_LN)):
            r("gpsimd", lambda g, tile=tile, val=val: g.memset(tile[:], val))
        for tile, pat, base, chm in (
                (iota_p, [[0, 1]], 0, 1),
                (pos_row, [[1, S]], 0, 0),
                (rev8, [[-1, NEXP]], NEXP, 0)):
            r("gpsimd", lambda g, tile=tile, pat=pat, base=base, chm=chm: g.iota(
                tile[:], pattern=pat, base=base, channel_multiplier=chm,
                allow_small_or_imprecise_dtypes=True))
        for i in range(32):
            fv = float(np.float32(10000.0 ** (-i / 32.0)))
            r("gpsimd", lambda g, i=i, fv=fv: g.memset(freqs[0:1, i:i + 1], fv))
        tgp = p.retro("gpsimd")
        with init_stk:
            if DEBUG_AA:
                dbg_ang = sb("i_dbga", [128, NT, 32], F32, init_stk)
                dbg_pang = sb("i_dbgp", [128, NT, 32], F32, init_stk)
            ang_r = sb("i_angr", [128, 32], F32, init_stk)
            tqi = sb("i_tqi", [128, 32], I32, init_stk)
            g1 = sb("i_g1", [128, 32], F32, init_stk)
            tq = sb("i_tq", [128, 32], F32, init_stk)
            sh = sb("i_sh", [128, 32], F32, init_stk)
            cb = sb("i_cb", [128, 32], F32, init_stk)
            sbs = sb("i_sb", [128, 32], F32, init_stk)
            pang_t = psum_("i_pang", [128, 512], F32, init_stk)
            pang_ps = pang_t[:, 0:32]
            angL = sb("i_angL", [128, 32], I32, init_stk)
            pang = angL.bitcast(F32)[:]
            for tt in range(NT):
                p.wait("tensor", tgp)
                tmm = r("tensor", lambda t_, tt=tt: t_.matmul(
                    pang_ps, pos_row[0:1, tt * 128:(tt + 1) * 128], freqs[:],
                    start=True, stop=True), inc="pe")
                p.wait("vector", tmm)
                r("vector", lambda v: v.tensor_copy(pang, pang_ps))
                r("vector", lambda v: v.tensor_scalar(
                    tq[:], pang, float(1.0 / (2 * np.pi)), 0.5, ALU.mult, ALU.add))
                r("vector", lambda v: v.tensor_copy(tqi[:], tq[:]))
                r("vector", lambda v: v.tensor_copy(tq[:], tqi[:]))
                r("vector", lambda v: v.scalar_tensor_tensor(
                    ang_r[:], tq[:], -TWO_PI_HI, pang, ALU.mult, ALU.add))
                r("vector", lambda v: v.tensor_single_scalar(g1[:], ang_r[:], float(np.pi), ALU.is_gt))
                r("vector", lambda v: v.tensor_add(tq[:], tq[:], g1[:]))
                r("vector", lambda v: v.tensor_single_scalar(g1[:], ang_r[:], float(-np.pi), ALU.is_lt))
                r("vector", lambda v: v.tensor_sub(tq[:], tq[:], g1[:]))
                r("vector", lambda v: v.scalar_tensor_tensor(
                    ang_r[:], tq[:], -TWO_PI_HI, pang, ALU.mult, ALU.add))
                r("vector", lambda v: v.tensor_scalar_mul(tq[:], tq[:], -TWO_PI_LO))
                r("vector", lambda v: v.tensor_add(ang_r[:], ang_r[:], tq[:]))
                r("vector", lambda v: v.tensor_scalar_min(ang_r[:], ang_r[:], float(np.pi)))
                tv = r("vector", lambda v: v.tensor_scalar_max(ang_r[:], ang_r[:], float(-np.pi)), inc="dv")
                p.wait("scalar", tv)
                r("scalar", lambda s_: s_.activation(sh[:], ang_r[:], AF.Sin, scale=0.5))
                ts2 = r("scalar", lambda s_: s_.activation(sbs[:], ang_r[:], AF.Sin), inc="ac")
                p.wait("vector", ts2)
                r("vector", lambda v: v.tensor_mul(cb[:], sh[:], sh[:]))
                r("vector", lambda v: v.tensor_scalar(cb[:], cb[:], -2.0, 1.0, ALU.mult, ALU.add))
                r("vector", lambda v, tt=tt: v.tensor_copy(
                    c2t[:, tt, :].rearrange("p (i two) -> p i two", two=2),
                    cb[:, :, None].broadcast_to([128, 32, 2])))
                r("vector", lambda v, tt=tt: v.tensor_scalar_mul(
                    s2t[:, tt, 0:64:2], sbs[:], -1.0))
                tv2 = r("vector", lambda v, tt=tt: v.tensor_copy(
                    s2t[:, tt, 1:64:2], sbs[:]), inc="dv")
                if DEBUG_AA:
                    r("vector", lambda v, tt=tt: v.tensor_copy(dbg_ang[:, tt, :], ang_r[:]))
                    tv2 = r("vector", lambda v, tt=tt: v.tensor_copy(
                        dbg_pang[:, tt, :], pang[:]), inc="dv")
                p.wait("tensor", tv2)
            if DEBUG_AA:
                p.barrier()
                p.dma("sync", lambda sy: sy.dma_start(dbg["d_ang"][:], dbg_ang[:]))
                p.dma("sync", lambda sy: sy.dma_start(dbg["d_pang"][:], dbg_pang[:]))
        p.barrier()

    def emit_ln(need_rstd1):
        with ExitStack() as stk:
            pst_sum = psum_("ln_sum", [1, 512], F32, stk)
            pst_sq = psum_("ln_sq", [1, 512], F32, stk)
            pbc = psum_("ln_bc", [128, 512], F32, stk)
            m_row = sb("m_row", [1, S], F32, stk)
            v_row = sb("v_row", [1, S], F32, stk)
            r_row = sb("r_row", [1, S], F32, stk)
            mr_row = sb("mr_row", [1, S], F32, stk)
            xb2 = sb("ln_xb", [128, 2, 512], BF16, stk)
            sqb = sb("ln_sqb", [128, 2, 512], BF16, stk)
            stat_rd = None
            for tg in range(NG):
                sl = slice(tg * 512, (tg + 1) * 512)
                last = None
                exact = need_rstd1 and tg == NG - 1
                if not exact:
                    mm_tok = [None, None]
                    for c in range(NCH):
                        buf = c % 2
                        p.wait("vector", mm_tok[buf])
                        tc = r("vector", lambda v, c=c, buf=buf, sl=sl: v.tensor_copy(
                            xb2[:, buf, :], x_fm[:, c, sl]), inc="dv")
                        p.wait("scalar", mm_tok[buf])
                        ts = r("scalar", lambda s_, c=c, buf=buf, sl=sl: s_.activation(
                            sqb[:, buf, :], x_fm[:, c, sl], AF.Square), inc="ac")
                        if c == 0:
                            p.wait("tensor", stat_rd)
                        p.wait("tensor", tc)
                        r("tensor", lambda t_, c=c, buf=buf: t_.matmul(
                            pst_sum[:], ones_col_b[:], xb2[:, buf, :],
                            start=(c == 0), stop=(c == NCH - 1)))
                        p.wait("tensor", ts)
                        mm_tok[buf] = r("tensor", lambda t_, c=c, buf=buf: t_.matmul(
                            pst_sq[:], ones_col_b[:], sqb[:, buf, :],
                            start=(c == 0), stop=(c == NCH - 1)), inc="pe")
                    last = mm_tok[(NCH - 1) % 2]
                else:
                    for c in range(NCH):
                        if c == 0:
                            p.wait("tensor", stat_rd)
                        r("tensor", lambda t_, c=c, sl=sl: t_.matmul(
                            pst_sum[:], ones_col[:], x_fm[:, c, sl],
                            start=(c == 0), stop=(c == NCH - 1)))
                        tsq = r("scalar", lambda s_, c=c, sl=sl: s_.activation(
                            sq_scr[:], x_fm[:, c, sl], AF.Square), inc="ac")
                        p.wait("tensor", tsq)
                        last = r("tensor", lambda t_, c=c: t_.matmul(
                            pst_sq[:], ones_col[:], sq_scr[:],
                            start=(c == 0), stop=(c == NCH - 1)), inc="pe")
                        p.wait("scalar", last)
                p.wait("vector", last)
                r("vector", lambda v, sl=sl: v.tensor_scalar_mul(m_row[0:1, sl], pst_sum[:], 1.0 / D))
                r("vector", lambda v, sl=sl: v.tensor_scalar_mul(v_row[0:1, sl], pst_sq[:], 1.0 / D))
                r("vector", lambda v, sl=sl: v.tensor_mul(r_row[0:1, sl], m_row[0:1, sl], m_row[0:1, sl]))
                tdv = r("vector", lambda v, sl=sl: v.tensor_sub(
                    v_row[0:1, sl], v_row[0:1, sl], r_row[0:1, sl]), inc="dv")
                stat_rd = tdv
                p.wait("scalar", tdv)
                r("scalar", lambda s_, sl=sl: s_.activation(
                    r_row[0:1, sl], v_row[0:1, sl], AF.Ln, bias=eps_sb[:]))
                tac = r("scalar", lambda s_, sl=sl: s_.activation(
                    r_row[0:1, sl], r_row[0:1, sl], AF.Exp, scale=-0.5), inc="ac")
                p.wait("vector", tac)
                tdv = r("vector", lambda v, sl=sl: v.tensor_mul(
                    mr_row[0:1, sl], m_row[0:1, sl], r_row[0:1, sl]), inc="dv")
                p.wait("tensor", tac)
                p.wait("tensor", tdv)
                tmm = r("tensor", lambda t_, sl=sl: t_.matmul(
                    pbc[:], ones_row[:], r_row[0:1, sl], start=True, stop=True), inc="pe")
                p.wait("vector", tmm)
                r("vector", lambda v, sl=sl: v.tensor_copy(rstd_b[:, sl], pbc[:]))
                if need_rstd1:
                    r("vector", lambda v, sl=sl: v.tensor_scalar_add(rstd1_b[0][:, sl], pbc[:], 1.0))
                    if tg == NG - 1:
                        r("vector", lambda v: v.tensor_copy(m_last[:], m_row[0:1, S - 1:S]))
                        r("vector", lambda v: v.tensor_copy(r_last[:], r_row[0:1, S - 1:S]))
                ta = p.retro("vector")
                p.wait("tensor", ta)
                tmm = r("tensor", lambda t_, sl=sl: t_.matmul(
                    pbc[:], ones_row[:], mr_row[0:1, sl], start=True, stop=True), inc="pe")
                p.wait("vector", tmm)
                ta = r("vector", lambda v, sl=sl: v.tensor_copy(mrstd_b[:, sl], pbc[:]), inc="dv")
                p.wait("tensor", ta)
            if DEBUG_AA and DBG_STATE.get("lnd") is None:
                DBG_STATE["lnd"] = True
                p.barrier()
                for nm, t_row in (("d_mrow", m_row), ("d_rrow", r_row), ("d_vrow", v_row)):
                    p.dma("sync", lambda sy, nm=nm, t_row=t_row: sy.dma_start(dbg[nm][:], t_row[:]))
                p.barrier()
            p.barrier()

    def emit_xn(tg, dep=None):
        sl = slice(tg * 512, (tg + 1) * 512)
        p.wait("vector", dep)
        tk = None
        for c in range(NCH):
            r("vector", lambda v, c=c, sl=sl: v.tensor_mul(
                sq_scr[:], x_fm[:, c, sl], rstd_b[:, sl]))
            tk = r("vector", lambda v, c=c, sl=sl: v.tensor_sub(
                xn_sb[:, c, :], sq_scr[:], mrstd_b[:, sl]), inc="dv")
        return tk

    def emit_attn_a():
        emit_ln(need_rstd1=False)
        with ExitStack() as stk:
            wA = sb("wA", [128, NCH, D], BF16, stk)
            wB = sb("wB", [128, NCH, D], BF16, stk)
            ba = sb("ba", [1, 2 * D], BF16, stk)
            bkt = ba[0:1, 0:D]
            bvt = ba[0:1, D:2 * D]
            bqt = ba[0:1, 0:D]
            k_sb = sb("k_sb", [128, NCH, S], BF16, stk)
            v_sb = sb("v_sb", [128, NT, 16 * 65], BF16, stk)
            q_sb = sb("q_sb", [128, NCH, 512], BF16, stk)
            qk_tok = sb("qk_tok", [128, D], BF16, stk)
            qsw = sb("qsw", [128, D], BF16, stk)
            rop_a = sb("rop_a", [128, D], BF16, stk)
            e_sb = sb("e_sb", [128, 2, 1024], BF16, stk)
            afm = xn_sb
            rrow = sq_scr[0:1, :]
            rb_s = sq_scr[64:128, :]

            pqk = psum_("pqk", [128, 512], F32, stk)
            ptr = psum_("ptr", [128, 1024], BF16, stk)
            pstb = [psum_("pst", [128, 2, 512], F32, stk) for _ in range(2)]
            ppvb = [psum_("ppv", [128, 512], F32, stk) for _ in range(2)]

            wtok = {}

            def load_w(nm, dst, src):
                for hf in range(2):
                    wtok[(nm, hf)] = p.dma("gpsimd", lambda g, dst=dst, src=src, hf=hf: g.dma_start(
                        dst[:, hf * 4:hf * 4 + 4, :],
                        src.rearrange("(c p) j -> p c j", p=128)[:, hf * 4:hf * 4 + 4, :]))
            for tt in range(NT):
                r("gpsimd", lambda g, tt=tt: g.memset(
                    v_sb[:, tt, :].rearrange("p (h d) -> p h d", d=65)[:, :, 64:65],
                    1.0))
            vones_tok = p.retro("gpsimd")
            p.wait("tensor", vones_tok)
            load_w("wk", wA, dr["aWkT"])
            load_w("wv", wB, dr["aWvT"])
            wtok["bk"] = p.dma("gpsimd", lambda g: g.dma_start(bkt, dr["abk"][:]))
            wtok["bv"] = p.dma("gpsimd", lambda g: g.dma_start(bvt, dr["abv"][:]))
            if DEBUG_AA and not DBG_STATE["dumped"]:
                DBG_STATE["dumped"] = True
                p.wait("sync", wtok[("wk", 0)])
                p.wait("sync", wtok[("wk", 1)])
                p.dma("sync", lambda sy: sy.dma_start(dbg["d_wa"][:], wA[:]))
                p.barrier()

            def proj_half(wname, wsb_, bias_sb, tt, tg, hf, free_tok, bank=None):
                bank = pqk if bank is None else bank
                t0 = (tt - tg * 4) * 128
                p.wait("tensor", free_tok)
                for c in range(NCH):
                    if c == 0:
                        p.wait("tensor", wtok[(wname, 0)])
                    if c == 4:
                        p.wait("tensor", wtok[(wname, 1)])
                    r("tensor", lambda t_, c=c, hf=hf, t0=t0, bank=bank: t_.matmul(
                        bank[:], xn_sb[:, c, t0:t0 + 128],
                        wsb_[:, c, hf * 512:(hf + 1) * 512],
                        start=(c == 0), stop=(bias_sb is None and c == NCH - 1)))
                if bias_sb is not None:
                    p.wait("tensor", wtok["b" + wname[1]])
                    boff = {"wk": 0, "wv": D, "wq": 0}[wname]
                    r("tensor", lambda t_, hf=hf, boff=boff, bank=bank: t_.matmul(
                        bank[:], ones_row_b[:], ba[0:1, boff + hf * 512:boff + (hf + 1) * 512],
                        start=False, stop=True))
                return p.retro("tensor")

            def produce_tok(wname, wsb_, bias_sb, dst_tile, tt, tg, free_tok, guard):
                banks = (pqk, ppvb[0])
                tmm = [None, None]
                for hf in range(2):
                    tmm[hf] = proj_half(wname, wsb_, bias_sb, tt, tg, hf,
                                        free_tok, bank=banks[hf])
                tev = None
                for hf in range(2):
                    p.wait("scalar", tmm[hf])
                    if hf == 0:
                        p.wait("scalar", guard)
                    tev = r("scalar", lambda s_, hf=hf: s_.copy(
                        dst_tile[:, hf * 512:(hf + 1) * 512], banks[hf][:]), inc="ac")
                return tev

            def rope(tt, src_tok):
                p.wait("vector", src_tok)
                r("vector", lambda v: v.tensor_copy(qsw[:, 0:D:2], qk_tok[:, 1:D:2]))
                r("vector", lambda v: v.tensor_copy(qsw[:, 1:D:2], qk_tok[:, 0:D:2]))
                c3 = c2t[:, tt, :][:, None, :].broadcast_to([128, 16, 64])
                s3 = s2t[:, tt, :][:, None, :].broadcast_to([128, 16, 64])
                r("vector", lambda v, c3=c3: v.tensor_tensor(
                    rop_a[:].rearrange("p (h d) -> p h d", h=16),
                    qk_tok[:].rearrange("p (h d) -> p h d", h=16), c3, ALU.mult))
                r("vector", lambda v, s3=s3: v.tensor_tensor(
                    qsw[:].rearrange("p (h d) -> p h d", h=16),
                    qsw[:].rearrange("p (h d) -> p h d", h=16), s3, ALU.mult))
                return r("vector", lambda v: v.tensor_add(rop_a[:], rop_a[:], qsw[:]), inc="dv")

            def transpose_pairs(dst_ap, rd_tok):
                p.wait("tensor", rd_tok)
                for hp in range(NCH):
                    r("tensor", lambda t_, hp=hp: t_.transpose(
                        ptr[:, hp * 128:(hp + 1) * 128],
                        rop_a[:, hp * 128:(hp + 1) * 128], ident_b[:]))
                ttr = p.retro("tensor")
                p.wait("vector", ttr)
                return r("vector", lambda v, dst_ap=dst_ap: v.tensor_copy(
                    dst_ap, ptr[:].rearrange("p (c t) -> p c t", c=NCH)), inc="dv")

            free_q = None
            xn_pe = None
            rope_guard = None
            for tg in range(NG):
                p.wait("vector", xn_pe)
                txn = emit_xn(tg, dep=None)
                p.wait("tensor", txn)
                for tl in range(4):
                    tt = tg * 4 + tl
                    if "a1_tile" in EXTRA_BARRIERS:
                        p.barrier()
                    tev = produce_tok("wk", wA, bkt, qk_tok, tt, tg, free_q, rope_guard)
                    trp = rope(tt, tev)
                    rope_guard = trp
                    if DEBUG_AA and tt == 0:
                        p.barrier()
                        for nm, t_sb in (("d_xn", xn_sb), ("d_c2t", c2t), ("d_s2t", s2t),
                                         ("d_kraw", qk_tok), ("d_krop", rop_a)):
                            p.dma("sync", lambda sy, nm=nm, t_sb=t_sb: sy.dma_start(
                                dbg[nm][:], t_sb[:]))
                        p.barrier()
                    transpose_pairs(k_sb[:, :, tt * 128:(tt + 1) * 128], trp)
                    vbanks = (pqk, ppvb[0])
                    tmmv = [None, None]
                    for hf in range(2):
                        tmmv[hf] = proj_half("wv", wB, bvt, tt, tg, hf, tev,
                                             bank=vbanks[hf])
                    tevv = None
                    for hf in range(2):
                        p.wait("scalar", tmmv[hf])
                        tevv = r("scalar", lambda s_, hf=hf, tt=tt: s_.copy(
                            v_sb[:, tt, hf * 520:(hf + 1) * 520].rearrange(
                                "p (h d) -> p h d", d=65)[:, :, 0:64],
                            vbanks[hf][:].rearrange("p (h d) -> p h d", d=64)), inc="ac")
                    free_q = tevv
                xn_pe = p.tok("pe")
            p.wait("gpsimd", xn_pe)
            load_w("wq", wA, dr["aWqT"])
            load_w("wo", wB, dr["aWoT"])
            wtok["bq"] = p.dma("gpsimd", lambda g: g.dma_start(bqt, dr["abq"][:]))
            o_read_tok = None
            exp_tok = [None, None]
            radd_tok = [None, None]
            pv_done = None
            for qg in range(NG):
                p.wait("vector", xn_pe)
                p.wait("vector", o_read_tok)
                txn = emit_xn(qg, dep=None)
                p.wait("tensor", txn)
                tev_t = None
                for tl in range(4):
                    tt = qg * 4 + tl
                    tev = produce_tok("wq", wA, bqt, qk_tok, tt, qg,
                                      free_q if tl == 0 else tev_t, rope_guard)
                    trp = rope(tt, tev)
                    rope_guard = trp
                    p.wait("vector", o_read_tok)
                    tev_t = transpose_pairs(q_sb[:, :, tl * 128:(tl + 1) * 128], trp)
                xn_pe = p.tok("pe")
                free_q = tev_t
                if "postQ" in EXTRA_BARRIERS:
                    p.barrier()
                for hp in range(NCH):
                    if "hp" in EXTRA_BARRIERS:
                        p.barrier()
                    hA, hB = 2 * hp, 2 * hp + 1

                    def emit_pv(kcp, hp=hp, hA=hA, hB=hB):
                        bufp = kcp % 2
                        if kcp == 0:
                            p.wait("tensor", pv_done)
                        p.wait("tensor", exp_tok[bufp])
                        st_fl, sp_fl = (kcp == 0), (kcp == NT - 1)
                        for hh, hx in ((0, hA), (1, hB)):
                            r("tensor", lambda t_, kcp=kcp, hx=hx, hh=hh, bufp=bufp, st_fl=st_fl, sp_fl=sp_fl: t_.matmul(
                                ppvb[hh][0:65, :], v_sb[:, kcp, hx * 65:(hx + 1) * 65],
                                e_sb[:, bufp, hh * 512:(hh + 1) * 512],
                                start=st_fl, stop=sp_fl, skip_group_check=True))

                    for kc in range(NT):
                        buf = kc % 2
                        p.wait("tensor", exp_tok[buf])
                        p.wait("tensor", tev_t)
                        r("tensor", lambda t_, hp=hp, kc=kc, buf=buf: t_.matmul(
                            pstb[buf][:, 0, :],
                            k_sb[0:64, hp, kc * 128:(kc + 1) * 128],
                            q_sb[0:64, hp, :], start=True, stop=True))
                        tst = r("tensor", lambda t_, hp=hp, kc=kc, buf=buf: t_.matmul(
                            pstb[buf][:, 1, :],
                            k_sb[64:128, hp, kc * 128:(kc + 1) * 128],
                            q_sb[64:128, hp, :], start=True, stop=True), inc="pe")
                        p.wait("scalar", tst)
                        exp_tok[buf] = r("scalar", lambda s_, buf=buf: s_.activation(
                            e_sb[:, buf, :], pstb[buf][:, :, :],
                            AF.Exp, scale=INV_SCALE_A), inc="ac")
                        if kc > 0:
                            emit_pv(kc - 1)
                    emit_pv(NT - 1)
                    tok_pv = p.retro("tensor")
                    prev_bc = None
                    nrm_tok = None
                    for hh in range(2):
                        pbase = hh * 64
                        p.wait("scalar", tok_pv)
                        p.wait("scalar", prev_bc)
                        r("scalar", lambda s_, hh=hh: s_.activation(
                            rrow, ppvb[hh][64:65, :], AF.Ln))
                        trc = r("scalar", lambda s_: s_.activation(
                            rrow, rrow, AF.Exp, scale=-1.0), inc="ac")
                        p.wait("tensor", trc)
                        tbc = r("tensor", lambda t_, hh=hh: t_.matmul(
                            ppvb[hh][64:128, :], ones_64[:], rrow,
                            start=True, stop=True, skip_group_check=True,
                            tile_position=(0, 64)), inc="pe")
                        prev_bc = tbc
                        p.wait("scalar", tbc)
                        tevb = r("scalar", lambda s_, hh=hh: s_.copy(
                            rb_s, ppvb[hh][64:128, :]), inc="ac")
                        p.wait("vector", tevb)
                        p.wait("vector", o_read_tok)
                        nrm_tok = r("vector", lambda v, hh=hh, pbase=pbase, hp=hp: v.tensor_mul(
                            afm[pbase:pbase + 64, hp, :], ppvb[hh][0:64, :],
                            rb_s), inc="dv")
                        p.wait("scalar", nrm_tok)
                    pv_done = nrm_tok
                if DEBUG_AA and qg == 0:
                    p.barrier()
                    for nm, t_sb in (("d_k", k_sb), ("d_v", v_sb), ("d_q", q_sb),
                                     ("d_afm", afm), ("d_e", e_sb), ("d_x", x_fm),
                                     ("d_rstd", rstd_b), ("d_mr", mrstd_b)):
                        p.dma("sync", lambda sy, nm=nm, t_sb=t_sb: sy.dma_start(
                            dbg[nm][:], t_sb[:]))
                    p.barrier()
                if "preO" in EXTRA_BARRIERS:
                    p.barrier()
                for jc in range(NCH):
                    buf = jc % 2
                    p.wait("tensor", radd_tok[buf])
                    p.wait("tensor", pv_done)
                    p.wait("tensor", exp_tok[buf])
                    for c in range(NCH):
                        if jc == 0 and c == 0:
                            p.wait("tensor", wtok[("wo", 0)])
                        if jc == 0 and c == 4:
                            p.wait("tensor", wtok[("wo", 1)])
                        r("tensor", lambda t_, c=c, jc=jc, buf=buf: t_.matmul(
                            pstb[buf][:, 0, :], wB[:, c, jc * 128:(jc + 1) * 128],
                            afm[:, c, :], start=(c == 0), stop=(c == NCH - 1)))
                    tok_o = p.retro("tensor")
                    o_read_tok = tok_o
                    p.wait("vector", tok_o)
                    sl = slice(qg * 512, (qg + 1) * 512)
                    radd_tok[buf] = r("vector", lambda v, jc=jc, buf=buf, sl=sl: v.tensor_add(
                        x_fm[:, jc, sl], x_fm[:, jc, sl], pstb[buf][:, 0, :]), inc="dv")
                exp_tok[0] = None
                exp_tok[1] = None
                pv_done = radd_tok[1] or radd_tok[0]
            p.barrier()

    def emit_ffn(stream_out=None):
        with ExitStack() as stk:
            rstd1_b[0] = sb("rstd1_b", [128, S], F32, stk)
            emit_ln(need_rstd1=True)
            w1s = sb("w1s", [128, NCH, FF], BF16, stk)
            w2s = sb("w2s", [128, NFC, D], BF16, stk)
            wof = sb("wof", [128, NCH, D], BF16, stk)
            wp1 = sb("wp1", [128, NCH, NAV], F32, stk)
            wp2 = sb("wp2", [128, NEXP], F32, stk)
            bp1 = sb("bp1", [1, NAV], F32, stk)
            bp2 = sb("bp2", [1, NEXP], F32, stk)
            b1f = sb("b1f", [128, NFC], F32, stk)
            b2f = sb("b2f", [128, NCH], F32, stk)
            bo3 = sb("bo3", [128, NCH], F32, stk)
            xl = sb("xl", [128, NCH], F32, stk)
            mb128 = sb("mb128", [128, 1], F32, stk)
            rb128 = sb("rb128", [128, 1], F32, stk)
            grow = sb("grow", [1, NAV], F32, stk)
            gcol = sb("gcol", [128, 1], F32, stk)
            lg = sb("lg", [1, NEXP], F32, stk)
            mx1 = sb("mx1", [1, 1], F32, stk)
            oh8 = sb("oh8", [1, NEXP], F32, stk)
            self_f = sb("self_f", [1, 1], F32, stk)
            base1 = sb("base1", [128, 1], F32, stk)
            idxw1 = sb("idxw1", [128, NCH], I32, stk)
            idxw2 = sb("idxw2", [128, NFC], I32, stk)
            idxb = sb("idxb", [128, 1], I32, stk)
            idxf_s = sb("idxf_s", [128, 1], F32, stk)
            h1_sb = sb("h1_sb", [128, NFC, 512], BF16, stk)
            rs_sb = xn_sb
            o3_sb = sb("o3_sb", [128, 512], F32, stk)
            xn32 = sb("xn32", [128, 512], F32, stk)

            with ExitStack() as rtk:
                prt_t = psum_("prt", [1, 512], F32, rtk)
                prt = prt_t[0:1, 0:128]
                prt8_t = psum_("prt8", [1, 512], F32, rtk)
                prt8 = prt8_t[0:1, 0:NEXP]
                pgt_t = psum_("pgt", [128, 512], F32, rtk)
                pgt = pgt_t[:, 0:1]
                pbs_t = psum_("pbs", [128, 512], F32, rtk)
                pbs = pbs_t[:, 0:1]

                t_wp1 = p.dma("sync", lambda sy: sy.dma_start(
                    wp1[:], dr["fWp1"].rearrange("(c p) f -> p c f", p=128)))
                t_wp2 = p.dma("sync", lambda sy: sy.dma_start(wp2[:], dr["fWp2"][:]))
                t_bp1 = p.dma("sync", lambda sy: sy.dma_start(bp1[:], dr["fbp1"][:]))
                t_bp2 = p.dma("sync", lambda sy: sy.dma_start(bp2[:], dr["fbp2"][:]))
                t_bo = p.dma("sync", lambda sy: sy.dma_start(bo3[:], dr["fbo_fm"][:]))
                twof = [p.dma("gpsimd", lambda g, hf=hf: g.dma_start(
                    wof[:, hf * 4:hf * 4 + 4, :],
                    dr["fWo"].rearrange("(c p) j -> p c j", p=128)[:, hf * 4:hf * 4 + 4, :]))
                    for hf in range(2)]

                tml = r("tensor", lambda t_: t_.matmul(
                    pgt, ones_row[:], m_last[:], start=True, stop=True))
                tml = r("tensor", lambda t_: t_.matmul(
                    pbs[:], ones_row[:], r_last[:], start=True, stop=True), inc="pe")
                p.wait("scalar", tml)
                r("scalar", lambda s_: s_.copy(mb128[:], pgt))
                tml2 = r("scalar", lambda s_: s_.copy(rb128[:], pbs), inc="ac")
                p.wait("vector", tml2)
                for c in range(NCH):
                    r("vector", lambda v, c=c: v.tensor_scalar(
                        xl[:, c:c + 1], x_fm[:, c, S - 1:S], mb128[:], rb128[:],
                        ALU.subtract, ALU.mult))
                txl = p.retro("vector")
                p.wait("tensor", txl)
                p.wait("tensor", t_wp1)
                for c in range(NCH):
                    r("tensor", lambda t_, c=c: t_.matmul(
                        prt, xl[:, c:c + 1], wp1[:, c, :],
                        start=(c == 0), stop=(c == NCH - 1)))
                tmm = p.retro("tensor")
                p.wait("vector", tmm)
                p.wait("vector", t_bp1)
                tdv = r("vector", lambda v: v.tensor_add(grow[:], prt, bp1[:]), inc="dv")
                p.wait("scalar", tdv)
                tac = r("scalar", lambda s_: s_.activation(grow[:], grow[:], AF.Gelu), inc="ac")
                p.wait("tensor", tac)
                tmm = r("tensor", lambda t_: t_.matmul(
                    pgt, grow[:], ones_1_f[:], start=True, stop=True), inc="pe")
                p.wait("scalar", tmm)
                tac = r("scalar", lambda s_: s_.copy(gcol[:], pgt), inc="ac")
                p.wait("tensor", tac)
                p.wait("tensor", t_wp2)
                tmm = r("tensor", lambda t_: t_.matmul(
                    prt8, gcol[:], wp2[:], start=True, stop=True), inc="pe")
                p.wait("vector", tmm)
                p.wait("vector", t_bp2)
                tdv = r("vector", lambda v: v.tensor_add(lg[:], prt8, bp2[:]), inc="dv")
                tdv = r("vector", lambda v: v.reduce_max(mx1[:], lg[:], axis=AX.X), inc="dv")
                p.wait("vector", tdv)
                r("vector", lambda v: v.tensor_scalar(oh8[:], lg[:], mx1[:], None, ALU.is_ge))
                r("vector", lambda v: v.tensor_mul(oh8[:], oh8[:], rev8[:]))
                tdv = r("vector", lambda v: v.reduce_max(self_f[:], oh8[:], axis=AX.X), inc="dv")
                p.wait("vector", tdv)
                tsel = r("vector", lambda v: v.tensor_scalar(
                    self_f[:], self_f[:], -1.0, float(NEXP), ALU.mult, ALU.add), inc="dv")
                p.wait("tensor", tsel)
                tmm = r("tensor", lambda t_: t_.matmul(
                    pbs, ones_row[:], self_f[:], start=True, stop=True), inc="pe")
                p.wait("vector", tmm)
                r("vector", lambda v: v.scalar_tensor_tensor(
                    base1[:], pbs, float(D), iota_p[:], ALU.mult, ALU.add))
                for c in range(NCH):
                    r("vector", lambda v, c=c: v.tensor_scalar_add(
                        idxf_s[:], base1[:], float(c * 128)))
                    r("vector", lambda v, c=c: v.tensor_copy(idxw1[:, c:c + 1], idxf_s[:]))
                r("vector", lambda v: v.scalar_tensor_tensor(
                    base1[:], pbs, float(FF), iota_p[:], ALU.mult, ALU.add))
                for c in range(NFC):
                    r("vector", lambda v, c=c: v.tensor_scalar_add(
                        idxf_s[:], base1[:], float(c * 128)))
                    r("vector", lambda v, c=c: v.tensor_copy(idxw2[:, c:c + 1], idxf_s[:]))
                r("vector", lambda v: v.scalar_tensor_tensor(
                    base1[:], pbs, 128.0, iota_p[:], ALU.mult, ALU.add))
                tidx = r("vector", lambda v: v.tensor_copy(idxb[:], base1[:]), inc="dv")
                p.wait("gpsimd", tidx)
                gtok_w1 = [p.dma("gpsimd", lambda g, c=c: g.indirect_dma_start(
                    w1s[:, c, :], None, dr["fW1"][:],
                    IndirectOffsetOnAxis(idxw1[:, c:c + 1], 0))) for c in range(NCH)]
                gtok_w2 = [p.dma("gpsimd", lambda g, c=c: g.indirect_dma_start(
                    w2s[:, c, :], None, dr["fW2"][:],
                    IndirectOffsetOnAxis(idxw2[:, c:c + 1], 0))) for c in range(NFC)]
                g_b1 = p.dma("gpsimd", lambda g: g.indirect_dma_start(
                    b1f[:], None, dr["fb1r"][:], IndirectOffsetOnAxis(idxb[:], 0)))
                g_b2 = p.dma("gpsimd", lambda g: g.indirect_dma_start(
                    b2f[:], None, dr["fb2r"][:], IndirectOffsetOnAxis(idxb[:], 0)))
                p.wait("vector", t_bo)
                tbo3 = r("vector", lambda v: v.tensor_scalar_mul(bo3[:], bo3[:], 0.3), inc="dv")

            with ExitStack() as mtk:
                phb = [psum_("ph", [128, 512], F32, mtk) for _ in range(2)]
                prb = [psum_("pr", [128, 512], F32, mtk) for _ in range(2)]
                pob = [psum_("po", [128, 512], F32, mtk) for _ in range(2)]

                xn_dep = None
                xn_pe = None
                h_free = [None, None]
                r_free = [None, None]
                o_free = [None, None]
                h_rd = None
                rs_rd = None
                for tg in range(NG):
                    sl = slice(tg * 512, (tg + 1) * 512)
                    p.wait("vector", xn_pe)
                    txn = emit_xn(tg, dep=None)
                    p.wait("tensor", txn)
                    gel_tok = None
                    for fc in range(NFC):
                        buf = fc % 2
                        p.wait("tensor", h_free[buf])
                        for c in range(NCH):
                            if fc == 0:
                                p.wait("tensor", gtok_w1[c])
                            r("tensor", lambda t_, c=c, fc=fc, buf=buf: t_.matmul(
                                phb[buf][:], w1s[:, c, fc * 128:(fc + 1) * 128],
                                xn_sb[:, c, :], start=(c == 0), stop=(c == NCH - 1)))
                        th = p.retro("tensor")
                        p.wait("scalar", th)
                        if fc == 0:
                            p.wait("scalar", g_b1)
                        p.wait("scalar", h_rd)
                        gel_tok = r("scalar", lambda s_, fc=fc, buf=buf: s_.activation(
                            h1_sb[:, fc, :], phb[buf][:], AF.Gelu,
                            bias=b1f[:, fc:fc + 1]), inc="ac")
                        h_free[buf] = gel_tok
                    xn_pe = p.tok("pe")
                    ev_tok = None
                    for jc in range(NCH):
                        buf = jc % 2
                        p.wait("tensor", r_free[buf])
                        p.wait("tensor", gel_tok)
                        for fc in range(NFC):
                            if jc == 0:
                                p.wait("tensor", gtok_w2[fc])
                            r("tensor", lambda t_, fc=fc, jc=jc, buf=buf: t_.matmul(
                                prb[buf][:], w2s[:, fc, jc * 128:(jc + 1) * 128],
                                h1_sb[:, fc, :], start=(fc == 0), stop=(fc == NFC - 1)))
                        tr_ = p.retro("tensor")
                        h_rd = tr_
                        p.wait("scalar", tr_)
                        if jc == 0:
                            p.wait("scalar", g_b2)
                        p.wait("scalar", rs_rd)
                        ev_tok = r("scalar", lambda s_, jc=jc, buf=buf: s_.activation(
                            rs_sb[:, jc, :], prb[buf][:], AF.Identity,
                            bias=b2f[:, jc:jc + 1]), inc="ac")
                        r_free[buf] = ev_tok
                    if DEBUG_FF and tg == 0 and not DBG_STATE.get("ffd"):
                        DBG_STATE["ffd"] = True
                        p.barrier()
                        for nm, t_sb in (("f_lg", lg), ("f_sel", self_f), ("f_xl", xl),
                                         ("f_g", grow), ("f_w1", w1s), ("f_w2", w2s),
                                         ("f_b1", b1f), ("f_h1", h1_sb), ("f_rs", rs_sb),
                                         ("f_idx", idxw1)):
                            p.dma("sync", lambda sy, nm=nm, t_sb=t_sb: sy.dma_start(
                                dbgf[nm][:], t_sb[:]))
                        p.barrier()
                    for jc2 in range(NCH):
                        buf = jc2 % 2
                        p.wait("tensor", o_free[buf])
                        p.wait("tensor", ev_tok)
                        for jc in range(NCH):
                            if jc2 == 0 and jc == 0:
                                p.wait("tensor", twof[0])
                            if jc2 == 0 and jc == 4:
                                p.wait("tensor", twof[1])
                            r("tensor", lambda t_, jc=jc, jc2=jc2, buf=buf: t_.matmul(
                                pob[buf][:], wof[:, jc, jc2 * 128:(jc2 + 1) * 128],
                                rs_sb[:, jc, :], start=(jc == 0), stop=(jc == NCH - 1)))
                        to_ = p.retro("tensor")
                        rs_rd = to_
                        p.wait("scalar", to_)
                        p.wait("scalar", tbo3)
                        tev = r("scalar", lambda s_, jc2=jc2, buf=buf: s_.activation(
                            o3_sb[:], pob[buf][:], AF.Identity,
                            bias=bo3[:, jc2:jc2 + 1], scale=0.3), inc="ac")
                        o_free[buf] = tev
                        p.wait("vector", tev)
                        r("vector", lambda v, jc2=jc2, sl=sl: v.tensor_mul(
                            xn32[:], x_fm[:, jc2, sl], rstd1_b[0][:, sl]))
                        r("vector", lambda v, jc2=jc2, sl=sl: v.tensor_sub(
                            xn32[:], xn32[:], mrstd_b[:, sl]))
                        tra = r("vector", lambda v, jc2=jc2, sl=sl: v.tensor_add(
                            x_fm[:, jc2, sl], xn32[:], o3_sb[:]), inc="dv")
                        p.wait("scalar", tra)
                    if stream_out is not None:
                        p.wait("sync", tra)
                        p.dma("sync", lambda sy, sl=sl: sy.dma_start(
                            stream_out[:, :, sl], x_fm[:, :, sl]))
                    xn_pe = rs_rd
                p.barrier()

    def emit_attn_b():
        emit_ln(need_rstd1=False)
        with ExitStack() as stk:
            wq = sb("wq", [128, NCH, D], BF16, stk)
            wk = sb("wk", [128, NCH, D], BF16, stk)
            wv = sb("wv", [128, NCH, D], BF16, stk)
            wo = sb("wo", [128, NCH, D], BF16, stk)
            bo_fm = sb("bo_fm", [128, NCH], F32, stk)
            q_tok = sb("q_tok", [128, D], BF16, stk)
            qs_tok = sb("qs_tok", [128, D], BF16, stk)
            mx16 = sb("mx16", [128, 16], F32, stk)
            sm16 = sb("sm16", [128, 16], F32, stk)
            kp_sb = sb("kp_sb", [128, 4, D], BF16, stk)
            vb_sb = sb("vb_sb", [128, 4, D], BF16, stk)
            qb_sb = sb("qb_sb", [128, NCH, 512], BF16, stk)
            cn_sb = sb("cn_sb", [128, NCH, 64], BF16, stk)
            csum = sb("csum", [128, NCH, 64], F32, stk)
            kssum = sb("kssum", [128, NCH], F32, stk)
            rks = sb("rks", [128, NCH], F32, stk)
            abfm = sb("abfm", [128, NCH, 512], BF16, stk)
            o3b = sb("o3b", [128, 512], F32, stk)
            t0_s = sb("t0_s", [128, D], BF16, stk)
            t1_s = sb("t1_s", [128, D], BF16, stk)

            pqk = psum_("pqkb", [128, 512], F32, stk)
            ptr = psum_("ptrb", [128, 1024], BF16, stk)
            pC = psum_("pC", [128, NCH, 64], F32, stk)
            pks_t = psum_("pks", [128, 512], F32, stk)
            pks = pks_t
            pao = psum_("pao", [128, NCH, 64], F32, stk)
            pob2 = [psum_("pob", [128, 512], F32, stk) for _ in range(2)]

            wtok = {}
            for nm, dst, src in (("wq", wq, dr["bWqT"]), ("wk", wk, dr["bWkT"]),
                                 ("wv", wv, dr["bWvT"]), ("wo", wo, dr["bWoT"])):
                for hf in range(2):
                    wtok[(nm, hf)] = p.dma("gpsimd", lambda g, dst=dst, src=src, hf=hf: g.dma_start(
                        dst[:, hf * 4:hf * 4 + 4, :],
                        src.rearrange("(c p) j -> p c j", p=128)[:, hf * 4:hf * 4 + 4, :]))
            t_bo = p.dma("gpsimd", lambda g: g.dma_start(bo_fm[:], dr["bbo_fm"][:]))

            def proj_half(wname, wsb_, tt, tg, hf, free_tok):
                t0 = (tt - tg * 4) * 128
                p.wait("tensor", free_tok)
                for c in range(NCH):
                    if c == 0:
                        p.wait("tensor", wtok[(wname, 0)])
                    if c == 4:
                        p.wait("tensor", wtok[(wname, 1)])
                    r("tensor", lambda t_, c=c, hf=hf, t0=t0: t_.matmul(
                        pqk[:], xn_sb[:, c, t0:t0 + 128],
                        wsb_[:, c, hf * 512:(hf + 1) * 512],
                        start=(c == 0), stop=(c == NCH - 1)))
                return p.retro("tensor")

            xn_pe = None
            free_q = None
            buck_tok = None
            tao_prev = None
            pao_ev = None
            o_read = None
            tkp_prev = None
            q_guard = None
            radd = [None, None]
            for tg in range(NG):
                sl = slice(tg * 512, (tg + 1) * 512)
                p.wait("vector", xn_pe)
                txn = emit_xn(tg, dep=None)
                p.wait("tensor", txn)
                for tl in range(4):
                    if "qkv" in SKIP_AB:
                        break
                    tt = tg * 4 + tl
                    tev = None
                    for hf in range(2):
                        tmm = proj_half("wq", wq, tt, tg, hf, free_q if hf == 0 else tev)
                        p.wait("scalar", tmm)
                        if hf == 0:
                            p.wait("scalar", q_guard)
                        tev = r("scalar", lambda s_, hf=hf: s_.copy(
                            q_tok[:, hf * 512:(hf + 1) * 512], pqk[:]), inc="ac")
                    tevk = None
                    for hf in range(2):
                        tmm = proj_half("wk", wk, tt, tg, hf, tev if hf == 0 else tevk)
                        p.wait("scalar", tmm)
                        p.wait("scalar", tkp_prev)
                        tevk = r("scalar", lambda s_, hf=hf: s_.activation(
                            t0_s[:, hf * 512:(hf + 1) * 512], pqk[:],
                            AF.Identity, scale=SCALE_B), inc="ac")
                    tevv = None
                    for hf in range(2):
                        tmm = proj_half("wv", wv, tt, tg, hf, tevk if hf == 0 else tevv)
                        p.wait("vector", tmm)
                        tevv = r("vector", lambda v, hf=hf, tl=tl: v.tensor_copy(
                            vb_sb[:, tl, hf * 512:(hf + 1) * 512], pqk[:]), inc="dv")
                    free_q = tevv
                    if "softmax" in SKIP_AB:
                        p.wait("vector", tev)
                        tqn = r("vector", lambda v: v.tensor_copy(qs_tok[:], q_tok[:]), inc="dv")
                    q3 = q_tok[:].rearrange("p (h d) -> p h d", h=16)
                    p.wait("vector", tev)
                    tm = None
                    if "softmax" not in SKIP_AB:
                        tm = r("vector", lambda v, q3=q3: v.reduce_max(
                        mx16[:], q3, axis=AX.X), inc="dv")
                    if "softmax" not in SKIP_AB:
                        p.wait("vector", tm)
                        tsb_ = r("vector", lambda v, q3=q3: v.tensor_tensor(
                            qs_tok[:].rearrange("p (h d) -> p h d", h=16), q3,
                            mx16[:, :, None].broadcast_to([128, 16, 64]),
                            ALU.subtract), inc="dv")
                        q_guard = tsb_
                        p.wait("scalar", tsb_)
                        te = r("scalar", lambda s_: s_.activation(
                            qs_tok[:], qs_tok[:], AF.Exp, scale=SCALE_B), inc="ac")
                        p.wait("vector", te)
                        ts = r("vector", lambda v: v.reduce_sum(
                            sm16[:], qs_tok[:].rearrange("p (h d) -> p h d", h=16),
                            axis=AX.X), inc="dv")
                        p.wait("vector", ts)
                        ts = r("vector", lambda v: v.reciprocal(sm16[:], sm16[:]), inc="dv")
                        p.wait("vector", ts)
                        tqn = r("vector", lambda v: v.tensor_tensor(
                            qs_tok[:].rearrange("p (h d) -> p h d", h=16),
                            qs_tok[:].rearrange("p (h d) -> p h d", h=16),
                            sm16[:, :, None].broadcast_to([128, 16, 64]),
                            ALU.mult), inc="dv")
                        q_guard = tqn
                    p.wait("tensor", tqn)
                    for hp in range(NCH):
                        r("tensor", lambda t_, hp=hp: t_.transpose(
                            ptr[:, hp * 128:(hp + 1) * 128],
                            qs_tok[:, hp * 128:(hp + 1) * 128], ident_b[:]))
                    ttr = p.retro("tensor")
                    p.wait("vector", ttr)
                    p.wait("vector", o_read)
                    r("vector", lambda v, tl=tl: v.tensor_copy(
                        qb_sb[:, :, tl * 128:(tl + 1) * 128],
                        ptr[:].rearrange("p (c t) -> p c t", c=NCH)))
                    r("vector", lambda v: v.tensor_scalar_min(t1_s[:], t0_s[:], 0.0))
                    tmin = r("vector", lambda v: v.tensor_scalar_max(t0_s[:], t0_s[:], 0.0), inc="dv")
                    p.wait("scalar", tmin)
                    texp = r("scalar", lambda s_: s_.activation(t1_s[:], t1_s[:], AF.Exp), inc="ac")
                    p.wait("vector", texp)
                    tkp = r("vector", lambda v, tl=tl: v.tensor_add(
                        kp_sb[:, tl, :], t0_s[:], t1_s[:]), inc="dv")
                    tkp_prev = tkp
                    free_q = tkp
                xn_pe = p.tok("pe")
                for ul in range(8):
                    if "buckets" in SKIP_AB:
                        p.wait("vector", o_read)
                        pao_ev = r("vector", lambda v, ul=ul: v.memset(
                            abfm[:, :, ul * 64:(ul + 1) * 64], 0.125), inc="dv")
                        buck_tok = pao_ev
                        continue
                    u = tg * 8 + ul
                    tl = ul // 2
                    base = (ul % 2) * 64
                    if u > 0:
                        p.wait("vector", tao_prev)
                        r("vector", lambda v: v.tensor_scalar_add(rks[:], kssum[:], EPS_B))
                        trk = r("vector", lambda v: v.reciprocal(rks[:], rks[:]), inc="dv")
                        tcn = r("vector", lambda v: v.tensor_tensor(
                            cn_sb[:], csum[:],
                            rks[:, :, None].broadcast_to([128, NCH, 64]),
                            ALU.mult), inc="dv")
                    p.wait("tensor", buck_tok)
                    p.wait("tensor", free_q)
                    for hh in range(H):
                        cb_ = (hh % 2) * 64
                        r("tensor", lambda t_, hh=hh, cb_=cb_, base=base, tl=tl: t_.matmul(
                            pC[cb_:cb_ + 64, hh // 2, :],
                            kp_sb[base:base + 64, tl, hh * 64:(hh + 1) * 64],
                            vb_sb[base:base + 64, tl, hh * 64:(hh + 1) * 64],
                            start=(hh < 2), stop=(hh >= H - 2),
                            skip_group_check=True,
                            tile_position=(base, cb_)))
                    for hh in range(H):
                        cb_ = (hh % 2) * 64
                        r("tensor", lambda t_, hh=hh, cb_=cb_, base=base, tl=tl: t_.matmul(
                            pks[cb_:cb_ + 64, hh // 2:hh // 2 + 1],
                            kp_sb[base:base + 64, tl, hh * 64:(hh + 1) * 64],
                            ones_col_b[base:base + 64, :],
                            start=(hh < 2), stop=(hh >= H - 2),
                            skip_group_check=True,
                            tile_position=(base, cb_)))
                    tmm = p.retro("tensor")
                    if u > 0:
                        p.wait("tensor", tcn)
                        p.wait("tensor", pao_ev)
                        for hh in range(H):
                            cb_ = (hh % 2) * 64
                            r("tensor", lambda t_, hh=hh, cb_=cb_, ul=ul: t_.matmul(
                                pao[cb_:cb_ + 64, hh // 2, :],
                                cn_sb[cb_:cb_ + 64, hh // 2, :],
                                qb_sb[cb_:cb_ + 64, hh // 2, ul * 64:(ul + 1) * 64],
                                start=(hh < 2), stop=(hh >= H - 2),
                                skip_group_check=True,
                                tile_position=(cb_, cb_)))
                        tao = p.retro("tensor")
                        tao_prev = tao
                        p.wait("vector", tao)
                        pao_ev = r("vector", lambda v, ul=ul: v.tensor_copy(
                            abfm[:, :, ul * 64:(ul + 1) * 64], pao[:]), inc="dv")
                    else:
                        p.wait("vector", o_read)
                        pao_ev = r("vector", lambda v, ul=ul: v.memset(
                            abfm[:, :, ul * 64:(ul + 1) * 64], 0.0), inc="dv")
                    p.wait("vector", tmm)
                    if u == 0:
                        r("vector", lambda v: v.tensor_copy(csum[:], pC[:]))
                        buck_tok = r("vector", lambda v: v.tensor_copy(kssum[:], pks[:, 0:NCH]), inc="dv")
                    else:
                        r("vector", lambda v: v.tensor_add(csum[:], csum[:], pC[:]))
                        buck_tok = r("vector", lambda v: v.tensor_add(kssum[:], kssum[:], pks[:, 0:NCH]), inc="dv")
                for jc2 in range(NCH):
                    if "oproj" in SKIP_AB:
                        break
                    buf = jc2 % 2
                    p.wait("tensor", radd[buf])
                    p.wait("tensor", pao_ev)
                    for c in range(NCH):
                        if tg == 0 and jc2 == 0 and c == 0:
                            p.wait("tensor", wtok[("wo", 0)])
                        if tg == 0 and jc2 == 0 and c == 4:
                            p.wait("tensor", wtok[("wo", 1)])
                        r("tensor", lambda t_, c=c, jc2=jc2, buf=buf: t_.matmul(
                            pob2[buf][:], wo[:, c, jc2 * 128:(jc2 + 1) * 128],
                            abfm[:, c, :], start=(c == 0), stop=(c == NCH - 1)))
                    to_ = p.retro("tensor")
                    o_read = to_
                    if "oev" in SKIP_AB:
                        radd[buf] = to_
                        continue
                    p.wait("scalar", to_)
                    p.wait("scalar", t_bo)
                    tev = r("scalar", lambda s_, jc2=jc2, buf=buf: s_.activation(
                        o3b[:], pob2[buf][:], AF.Identity,
                        bias=bo_fm[:, jc2:jc2 + 1]), inc="ac")
                    if "oradd" in SKIP_AB:
                        radd[buf] = tev
                        continue
                    p.wait("vector", tev)
                    radd[buf] = r("vector", lambda v, jc2=jc2, sl=sl: v.tensor_add(
                        x_fm[:, jc2, sl], x_fm[:, jc2, sl], o3b[:]), inc="dv")
                    p.wait("scalar", radd[buf])
            p.barrier()

    ones_1_f = sb("ones_1f", [1, 1])

    r("gpsimd", lambda g: g.memset(ones_1_f[:], 1.0))
    emit_init()
    for seq in range(nseq):
        p.dma("sync", lambda sy, seq=seq: sy.dma_start(
            x_fm[:], xin[seq].rearrange("(c p) t -> p c t", p=128)))
        p.barrier()
        yout_fm = yout[seq].rearrange("(c p) t -> p c t", p=128)
        for i, phase in enumerate(phases):
            last = i == len(phases) - 1
            if phase == "attn_a":
                emit_attn_a()
            elif phase == "ffn":
                emit_ffn(stream_out=yout_fm if last else None)
            elif phase == "attn_b":
                emit_attn_b()
        if phases[-1] != "ffn":
            p.dma("sync", lambda sy, seq=seq: sy.dma_start(
                yout[seq].rearrange("(c p) t -> p c t", p=128), x_fm[:]))
        p.barrier()

    with nc.Block() as block:
        @block.tensor
        def _(e):
            for fn in p.streams["tensor"]:
                fn(e)

        @block.vector
        def _(e):
            for fn in p.streams["vector"]:
                fn(e)

        @block.scalar
        def _(e):
            for fn in p.streams["scalar"]:
                fn(e)

        @block.gpsimd
        def _(e):
            for fn in p.streams["gpsimd"]:
                fn(e)

        @block.sync
        def _(e):
            for fn in p.streams["sync"]:
                fn(e)
    es.close()
    return nc


def make_inputs(core, nseq, **w):
    xt = np.ascontiguousarray(
        w["L0_attn_a_x"][core * nseq:(core + 1) * nseq].transpose(0, 2, 1))

    def T(a):
        return np.ascontiguousarray(a.T)
    return {
        "xin": xt.astype(np.float32),
        "aWqT": T(w["L0_attn_a_Wq"]), "aWkT": T(w["L0_attn_a_Wk"]),
        "aWvT": T(w["L0_attn_a_Wv"]), "aWoT": T(w["L0_attn_a_Wo"]),
        "abq": w["L0_attn_a_bq"][None, :], "abk": w["L0_attn_a_bk"][None, :],
        "abv": w["L0_attn_a_bv"][None, :],
        "bWqT": T(w["L0_attn_b_Wq"]), "bWkT": T(w["L0_attn_b_Wk"]),
        "bWvT": T(w["L0_attn_b_Wv"]), "bWoT": T(w["L0_attn_b_Wo"]),
        "bbo_fm": np.ascontiguousarray(w["L0_attn_b_bo"].reshape(NCH, 128).T),
        "fWp1": w["L0_ffn_Wp1"], "fbp1": w["L0_ffn_bp1"][None, :],
        "fWp2": w["L0_ffn_Wp2"], "fbp2": w["L0_ffn_bp2"][None, :],
        "fW1": w["L0_ffn_W1"].reshape(NEXP * D, FF),
        "fW2": w["L0_ffn_W2"].reshape(NEXP * FF, D),
        "fb1r": np.ascontiguousarray(
            w["L0_ffn_b1"].reshape(NEXP, NFC, 128).transpose(0, 2, 1)).reshape(NEXP * 128, NFC),
        "fb2r": np.ascontiguousarray(
            w["L0_ffn_b2"].reshape(NEXP, NCH, 128).transpose(0, 2, 1)).reshape(NEXP * 128, NCH),
        "fWo": w["L0_ffn_Wo"],
        "fbo_fm": np.ascontiguousarray(w["L0_ffn_bo"].reshape(NCH, 128).T),
    }


def kernel(**inputs):
    inputs = {k: np.asarray(v, dtype=np.float32) for k, v in inputs.items()}
    ncores = 8
    nc = build_program()
    in_maps = [make_inputs(core, NSEQ, **inputs) for core in range(ncores)]
    res = run_bass_kernel_spmd(nc, in_maps, core_ids=list(range(ncores)))
    out = np.empty((B, S, D), dtype=np.float32)
    for core in range(ncores):
        y = np.asarray(res.results[core]["yout"])
        out[core * NSEQ:(core + 1) * NSEQ] = y.transpose(0, 2, 1)
    return out


# version 12 (final)
# speedup vs baseline: 1.4713x; 1.0066x over previous
import numpy as np
from contextlib import ExitStack

import concourse.bass as bass
import concourse.mybir as mybir
from concourse.bass_utils import run_bass_kernel_spmd
from concourse.bass_core_rust import IndirectOffsetOnAxis

F32 = mybir.dt.float32
BF16 = mybir.dt.bfloat16
I32 = mybir.dt.int32
ALU = mybir.AluOpType
AX = mybir.AxisListType
AF = mybir.ActivationFunctionType

B, S, D = 16, 2048, 1024
H, HD = 16, 64
NSEQ = 2
NT = S // 128
NG = S // 512
NCH = D // 128
FF = 2048
NFC = FF // 128
NEXP, NAV = 8, 128
INV_SCALE_A = 1.0 / 32.0
SCALE_B = float(64 ** -0.25)
EPS_LN = 1e-5
EPS_B = 1e-6
TWO_PI_HI = float(np.float32(2 * np.pi))
TWO_PI_LO = float(2 * np.pi - np.float64(np.float32(2 * np.pi)))
LN_THETA_SC = float(-2.0 * np.log(10000.0) / 64.0)

DEBUG_AA = False
DEBUG_FF = False
SKIP_AB = set()
EXTRA_BARRIERS = set()
DBG_STATE = {"dumped": False}

ENGINES = ("tensor", "vector", "scalar", "gpsimd", "sync")
ESEM = {"tensor": "pe", "vector": "dv", "scalar": "ac", "gpsimd": "gp", "sync": "sp"}
QSEM = {"gpsimd": "qg", "sync": "qs", "scalar": "qa"}


class Prog:
    def __init__(self, nc, es):
        self.nc = nc
        self.es = es
        self.streams = {e: [] for e in ENGINES}
        self.semh = {}
        self.semv = {}
        self.waited = {e: {} for e in ENGINES}
        self.last_cell = {e: None for e in ENGINES}
        self.last_is_dma = {e: False for e in ENGINES}
        self.last_tok = {e: None for e in ENGINES}
        self.active = {e: False for e in ENGINES}
        self.dma_rr = {e: 0 for e in ENGINES}
        self.uid = 0

    def mksem(self, name):
        self.semh[name] = self.es.enter_context(self.nc.semaphore(name))
        self.semv[name] = 0

    def tok(self, name):
        return (name, self.semv[name])

    def wait(self, eng, tok):
        if tok is None:
            return
        name, val = tok
        if val <= 0 or self.waited[eng].get(name, 0) >= val:
            return
        self.waited[eng][name] = val
        h = self.semh[name]
        self.streams[eng].append(lambda e, h=h, v=val: e.wait_ge(h, v))

    def run(self, eng, fn, inc=None, amt=1, _dma=False):
        if eng in ("vector", "scalar", "gpsimd") and not _dma:
            name = ESEM[eng]
            prev = self.semv[name]
            if prev > 0:
                self.wait(eng, (name, prev))
            inc = name
            amt = 1
        tok = None
        h = None
        if inc is not None:
            self.semv[inc] += amt
            tok = (inc, self.semv[inc])
            h = self.semh[inc]
        cell = [None]
        self.last_cell[eng] = cell
        self.last_is_dma[eng] = _dma
        self.active[eng] = True

        self.last_tok[eng] = tok

        def thunk(e, fn=fn, h=h, amt=amt, cell=cell):
            ins = fn(e)
            cell[0] = ins
            if h is not None:
                ins.then_inc(h, amt)
        self.streams[eng].append(thunk)
        return tok

    DMA_POOL = {"gpsimd": 8, "sync": 6, "scalar": 2}

    def dma(self, eng, fn):
        q = QSEM[eng]
        k = self.DMA_POOL[eng]
        j = self.dma_rr[eng] % k
        self.dma_rr[eng] += 1
        sem = f"{q}{j}"
        self.wait(eng, (sem, self.semv[sem]))
        return self.run(eng, fn, inc=sem, amt=16, _dma=True)

    def retro(self, eng):
        if self.last_tok[eng] is not None:
            return self.last_tok[eng]
        name = ESEM[eng]
        cell = self.last_cell[eng]
        assert cell is not None, f"retro on {eng} with no instructions yet"
        amt = 16 if self.last_is_dma[eng] else 1
        self.semv[name] += amt
        tok = (name, self.semv[name])
        h = self.semh[name]

        def thunk(e, cell=cell, h=h, amt=amt):
            cell[0].then_inc(h, amt)
        self.streams[eng].append(thunk)
        self.last_tok[eng] = tok
        return tok

    def barrier(self):
        toks = []
        for eng in ENGINES:
            if not self.active[eng]:
                continue
            q = QSEM.get(eng)
            if q is not None:
                for j in range(self.DMA_POOL[eng]):
                    sem = f"{q}{j}"
                    if self.semv[sem] > 0:
                        self.wait(eng, self.tok(sem))
            toks.append(self.retro(eng))
        for eng in ENGINES:
            for tk in toks:
                self.wait(eng, tk)
            self.active[eng] = False


def build_program(phases=("attn_a", "ffn", "attn_b", "ffn"), nseq=NSEQ):
    nc = bass.Bass("TRN2", target_bir_lowering=False)

    xin = nc.dram_tensor("xin", [nseq, D, S], F32, kind="ExternalInput")
    yout = nc.dram_tensor("yout", [nseq, D, S], F32, kind="ExternalOutput")
    if DEBUG_FF:
        dbgf = {
            "f_lg": nc.dram_tensor("f_lg", [1, NEXP], F32, kind="ExternalOutput"),
            "f_sel": nc.dram_tensor("f_sel", [1, 1], F32, kind="ExternalOutput"),
            "f_xl": nc.dram_tensor("f_xl", [128, NCH], F32, kind="ExternalOutput"),
            "f_g": nc.dram_tensor("f_g", [1, NAV], F32, kind="ExternalOutput"),
            "f_w1": nc.dram_tensor("f_w1", [128, NCH, FF], BF16, kind="ExternalOutput"),
            "f_w2": nc.dram_tensor("f_w2", [128, NFC, D], BF16, kind="ExternalOutput"),
            "f_b1": nc.dram_tensor("f_b1", [128, NFC], F32, kind="ExternalOutput"),
            "f_h1": nc.dram_tensor("f_h1", [128, NFC, 512], BF16, kind="ExternalOutput"),
            "f_rs": nc.dram_tensor("f_rs", [128, NCH, 512], BF16, kind="ExternalOutput"),
            "f_idx": nc.dram_tensor("f_idx", [128, NCH], I32, kind="ExternalOutput"),
        }
    if DEBUG_AA:
        dbg = {
            "d_k": nc.dram_tensor("d_k", [128, NCH, S], BF16, kind="ExternalOutput"),
            "d_v": nc.dram_tensor("d_v", [128, NT, D], BF16, kind="ExternalOutput"),
            "d_q": nc.dram_tensor("d_q", [128, NCH, 512], BF16, kind="ExternalOutput"),
            "d_afm": nc.dram_tensor("d_afm", [128, NCH, 512], BF16, kind="ExternalOutput"),
            "d_e": nc.dram_tensor("d_e", [128, 2, 1024], BF16, kind="ExternalOutput"),
            "d_x": nc.dram_tensor("d_x", [128, NCH, S], F32, kind="ExternalOutput"),
            "d_rstd": nc.dram_tensor("d_rstd", [128, S], BF16, kind="ExternalOutput"),
            "d_mr": nc.dram_tensor("d_mr", [128, S], BF16, kind="ExternalOutput"),
            "d_xn": nc.dram_tensor("d_xn", [128, NCH, 512], BF16, kind="ExternalOutput"),
            "d_wa": nc.dram_tensor("d_wa", [128, NCH, D], BF16, kind="ExternalOutput"),
            "d_kraw": nc.dram_tensor("d_kraw", [128, D], BF16, kind="ExternalOutput"),
            "d_krop": nc.dram_tensor("d_krop", [128, D], BF16, kind="ExternalOutput"),
            "d_mrow": nc.dram_tensor("d_mrow", [1, S], F32, kind="ExternalOutput"),
            "d_rrow": nc.dram_tensor("d_rrow", [1, S], F32, kind="ExternalOutput"),
            "d_vrow": nc.dram_tensor("d_vrow", [1, S], F32, kind="ExternalOutput"),
            "d_c2t": nc.dram_tensor("d_c2t", [128, NT, 64], BF16, kind="ExternalOutput"),
            "d_s2t": nc.dram_tensor("d_s2t", [128, NT, 64], BF16, kind="ExternalOutput"),
            "d_ang": nc.dram_tensor("d_ang", [128, NT, 32], F32, kind="ExternalOutput"),
            "d_pang": nc.dram_tensor("d_pang", [128, NT, 32], F32, kind="ExternalOutput"),
        }
    dr = {}
    for name, shape in [
        ("aWqT", [D, D]), ("aWkT", [D, D]), ("aWvT", [D, D]), ("aWoT", [D, D]),
        ("abq", [1, D]), ("abk", [1, D]), ("abv", [1, D]),
        ("bWqT", [D, D]), ("bWkT", [D, D]), ("bWvT", [D, D]), ("bWoT", [D, D]),
        ("bbo_fm", [128, NCH]),
        ("fWp1", [D, NAV]), ("fbp1", [1, NAV]), ("fWp2", [NAV, NEXP]),
        ("fbp2", [1, NEXP]),
        ("fW1", [NEXP * D, FF]), ("fW2", [NEXP * FF, D]),
        ("fb1r", [NEXP * 128, NFC]), ("fb2r", [NEXP * 128, NCH]),
        ("fWo", [D, D]), ("fbo_fm", [128, NCH]),
    ]:
        dr[name] = nc.dram_tensor(name, shape, F32, kind="ExternalInput")

    es = ExitStack()
    p = Prog(nc, es)
    for s in ("pe", "dv", "ac", "gp", "sp"):
        p.mksem(s)
    for q, eng in (("qg", "gpsimd"), ("qs", "sync"), ("qa", "scalar")):
        for j in range(Prog.DMA_POOL[eng]):
            p.mksem(f"{q}{j}")
    r = p.run

    def sb(name, shape, dt=F32, stk=es):
        p.uid += 1
        return stk.enter_context(nc.sbuf_tensor(f"{name}_{p.uid}", shape, dt))

    def psum_(name, shape, dt=F32, stk=es):
        p.uid += 1
        return stk.enter_context(nc.psum_tensor(f"{name}_{p.uid}", shape, dt))

    x_fm = sb("x_fm", [128, NCH, S])
    ident_b = sb("ident_b", [128, 128], BF16)
    ones_row = sb("ones_row", [1, 128])
    ones_row_b = sb("ones_row_b", [1, 128], BF16)
    ones_col = sb("ones_col", [128, 1])
    ones_col_b = sb("ones_col_b", [128, 1], BF16)
    ones_64 = sb("ones_64", [1, 64])
    eps_sb = sb("eps_sb", [1, 1])
    iota_p = sb("iota_p", [128, 1])
    c2t = sb("c2t", [128, NT, 64], BF16)
    s2t = sb("s2t", [128, NT, 64], BF16)
    rev8 = sb("rev8", [1, NEXP])
    rstd_b = sb("rstd_b", [128, S], BF16)
    mrstd_b = sb("mrstd_b", [128, S], BF16)
    rstd1_b = [None]
    m_last = sb("m_last", [1, 1])
    r_last = sb("r_last", [1, 1])
    sq_scr = sb("sq_scr", [128, 512])
    xn_sb = sb("xn_sb", [128, NCH, 512], BF16)
    pos_row = None
    freqs = None
    init_stk = ExitStack()

    def emit_init():
        nonlocal pos_row, freqs
        pos_row = sb("pos_row", [1, S], F32, init_stk)
        freqs = sb("freqs", [1, 32], F32, init_stk)
        r("gpsimd", lambda g: g.memset(ident_b[:], 0.0))
        r("gpsimd", lambda g: g.affine_select(
            out=ident_b[:], in_=ident_b[:], compare_op=ALU.not_equal,
            fill=1.0, base=0, pattern=[[-1, 128]], channel_multiplier=1))
        for tile, val in ((ones_row, 1.0), (ones_row_b, 1.0), (ones_col, 1.0),
                          (ones_col_b, 1.0), (ones_64, 1.0), (eps_sb, EPS_LN)):
            r("gpsimd", lambda g, tile=tile, val=val: g.memset(tile[:], val))
        for tile, pat, base, chm in (
                (iota_p, [[0, 1]], 0, 1),
                (pos_row, [[1, S]], 0, 0),
                (rev8, [[-1, NEXP]], NEXP, 0)):
            r("gpsimd", lambda g, tile=tile, pat=pat, base=base, chm=chm: g.iota(
                tile[:], pattern=pat, base=base, channel_multiplier=chm,
                allow_small_or_imprecise_dtypes=True))
        for i in range(32):
            fv = float(np.float32(10000.0 ** (-i / 32.0)))
            r("gpsimd", lambda g, i=i, fv=fv: g.memset(freqs[0:1, i:i + 1], fv))
        tgp = p.retro("gpsimd")
        with init_stk:
            if DEBUG_AA:
                dbg_ang = sb("i_dbga", [128, NT, 32], F32, init_stk)
                dbg_pang = sb("i_dbgp", [128, NT, 32], F32, init_stk)
            ang_r = sb("i_angr", [128, 32], F32, init_stk)
            tqi = sb("i_tqi", [128, 32], I32, init_stk)
            g1 = sb("i_g1", [128, 32], F32, init_stk)
            tq = sb("i_tq", [128, 32], F32, init_stk)
            sh = sb("i_sh", [128, 32], F32, init_stk)
            cb = sb("i_cb", [128, 32], F32, init_stk)
            sbs = sb("i_sb", [128, 32], F32, init_stk)
            pang_t = psum_("i_pang", [128, 512], F32, init_stk)
            pang_ps = pang_t[:, 0:32]
            angL = sb("i_angL", [128, 32], I32, init_stk)
            pang = angL.bitcast(F32)[:]
            for tt in range(NT):
                p.wait("tensor", tgp)
                tmm = r("tensor", lambda t_, tt=tt: t_.matmul(
                    pang_ps, pos_row[0:1, tt * 128:(tt + 1) * 128], freqs[:],
                    start=True, stop=True), inc="pe")
                p.wait("vector", tmm)
                r("vector", lambda v: v.tensor_copy(pang, pang_ps))
                r("vector", lambda v: v.tensor_scalar(
                    tq[:], pang, float(1.0 / (2 * np.pi)), 0.5, ALU.mult, ALU.add))
                r("vector", lambda v: v.tensor_copy(tqi[:], tq[:]))
                r("vector", lambda v: v.tensor_copy(tq[:], tqi[:]))
                r("vector", lambda v: v.scalar_tensor_tensor(
                    ang_r[:], tq[:], -TWO_PI_HI, pang, ALU.mult, ALU.add))
                r("vector", lambda v: v.tensor_single_scalar(g1[:], ang_r[:], float(np.pi), ALU.is_gt))
                r("vector", lambda v: v.tensor_add(tq[:], tq[:], g1[:]))
                r("vector", lambda v: v.tensor_single_scalar(g1[:], ang_r[:], float(-np.pi), ALU.is_lt))
                r("vector", lambda v: v.tensor_sub(tq[:], tq[:], g1[:]))
                r("vector", lambda v: v.scalar_tensor_tensor(
                    ang_r[:], tq[:], -TWO_PI_HI, pang, ALU.mult, ALU.add))
                r("vector", lambda v: v.tensor_scalar_mul(tq[:], tq[:], -TWO_PI_LO))
                r("vector", lambda v: v.tensor_add(ang_r[:], ang_r[:], tq[:]))
                r("vector", lambda v: v.tensor_scalar_min(ang_r[:], ang_r[:], float(np.pi)))
                tv = r("vector", lambda v: v.tensor_scalar_max(ang_r[:], ang_r[:], float(-np.pi)), inc="dv")
                p.wait("scalar", tv)
                r("scalar", lambda s_: s_.activation(sh[:], ang_r[:], AF.Sin, scale=0.5))
                ts2 = r("scalar", lambda s_: s_.activation(sbs[:], ang_r[:], AF.Sin), inc="ac")
                p.wait("vector", ts2)
                r("vector", lambda v: v.tensor_mul(cb[:], sh[:], sh[:]))
                r("vector", lambda v: v.tensor_scalar(cb[:], cb[:], -2.0, 1.0, ALU.mult, ALU.add))
                r("vector", lambda v, tt=tt: v.tensor_copy(
                    c2t[:, tt, :].rearrange("p (i two) -> p i two", two=2),
                    cb[:, :, None].broadcast_to([128, 32, 2])))
                r("vector", lambda v, tt=tt: v.tensor_scalar_mul(
                    s2t[:, tt, 0:64:2], sbs[:], -1.0))
                tv2 = r("vector", lambda v, tt=tt: v.tensor_copy(
                    s2t[:, tt, 1:64:2], sbs[:]), inc="dv")
                if DEBUG_AA:
                    r("vector", lambda v, tt=tt: v.tensor_copy(dbg_ang[:, tt, :], ang_r[:]))
                    tv2 = r("vector", lambda v, tt=tt: v.tensor_copy(
                        dbg_pang[:, tt, :], pang[:]), inc="dv")
                p.wait("tensor", tv2)
            if DEBUG_AA:
                p.barrier()
                p.dma("sync", lambda sy: sy.dma_start(dbg["d_ang"][:], dbg_ang[:]))
                p.dma("sync", lambda sy: sy.dma_start(dbg["d_pang"][:], dbg_pang[:]))
        p.barrier()

    def emit_ln(need_rstd1):
        with ExitStack() as stk:
            pst_sum = psum_("ln_sum", [1, 512], F32, stk)
            pst_sq = psum_("ln_sq", [1, 512], F32, stk)
            pbc = psum_("ln_bc", [128, 512], F32, stk)
            m_row = sb("m_row", [1, S], F32, stk)
            v_row = sb("v_row", [1, S], F32, stk)
            r_row = sb("r_row", [1, S], F32, stk)
            mr_row = sb("mr_row", [1, S], F32, stk)
            xb2 = sb("ln_xb", [128, 2, 512], BF16, stk)
            sqb = sb("ln_sqb", [128, 2, 512], BF16, stk)
            rb_t = sb("ln_rbt", [1, 512], BF16, stk)
            mrb_t = sb("ln_mrbt", [1, 512], BF16, stk)
            stat_rd = None
            for tg in range(NG):
                sl = slice(tg * 512, (tg + 1) * 512)
                last = None
                exact = need_rstd1 and tg == NG - 1
                if not exact:
                    mm_tok = [None, None]
                    for c in range(NCH):
                        buf = c % 2
                        p.wait("vector", mm_tok[buf])
                        tc = r("vector", lambda v, c=c, buf=buf, sl=sl: v.tensor_copy(
                            xb2[:, buf, :], x_fm[:, c, sl]), inc="dv")
                        p.wait("scalar", mm_tok[buf])
                        ts = r("scalar", lambda s_, c=c, buf=buf, sl=sl: s_.activation(
                            sqb[:, buf, :], x_fm[:, c, sl], AF.Square), inc="ac")
                        if c == 0:
                            p.wait("tensor", stat_rd)
                        p.wait("tensor", tc)
                        r("tensor", lambda t_, c=c, buf=buf: t_.matmul(
                            pst_sum[:], ones_col_b[:], xb2[:, buf, :],
                            start=(c == 0), stop=(c == NCH - 1)))
                        p.wait("tensor", ts)
                        mm_tok[buf] = r("tensor", lambda t_, c=c, buf=buf: t_.matmul(
                            pst_sq[:], ones_col_b[:], sqb[:, buf, :],
                            start=(c == 0), stop=(c == NCH - 1)), inc="pe")
                    last = mm_tok[(NCH - 1) % 2]
                else:
                    for c in range(NCH):
                        if c == 0:
                            p.wait("tensor", stat_rd)
                        r("tensor", lambda t_, c=c, sl=sl: t_.matmul(
                            pst_sum[:], ones_col[:], x_fm[:, c, sl],
                            start=(c == 0), stop=(c == NCH - 1)))
                        tsq = r("scalar", lambda s_, c=c, sl=sl: s_.activation(
                            sq_scr[:], x_fm[:, c, sl], AF.Square), inc="ac")
                        p.wait("tensor", tsq)
                        last = r("tensor", lambda t_, c=c: t_.matmul(
                            pst_sq[:], ones_col[:], sq_scr[:],
                            start=(c == 0), stop=(c == NCH - 1)), inc="pe")
                        p.wait("scalar", last)
                p.wait("vector", last)
                r("vector", lambda v, sl=sl: v.tensor_scalar_mul(m_row[0:1, sl], pst_sum[:], 1.0 / D))
                r("vector", lambda v, sl=sl: v.tensor_scalar_mul(v_row[0:1, sl], pst_sq[:], 1.0 / D))
                r("vector", lambda v, sl=sl: v.tensor_mul(r_row[0:1, sl], m_row[0:1, sl], m_row[0:1, sl]))
                tdv = r("vector", lambda v, sl=sl: v.tensor_sub(
                    v_row[0:1, sl], v_row[0:1, sl], r_row[0:1, sl]), inc="dv")
                stat_rd = tdv
                p.wait("scalar", tdv)
                r("scalar", lambda s_, sl=sl: s_.activation(
                    r_row[0:1, sl], v_row[0:1, sl], AF.Ln, bias=eps_sb[:]))
                if not exact:
                    tac = r("scalar", lambda s_, sl=sl: s_.activation(
                        rb_t[:], r_row[0:1, sl], AF.Exp, scale=-0.5), inc="ac")
                    p.wait("vector", tac)
                    tdv = r("vector", lambda v, sl=sl: v.tensor_mul(
                        mrb_t[:], m_row[0:1, sl], rb_t[:]), inc="dv")
                    r_bc, mr_bc, ones_bc = rb_t[:], mrb_t[:], ones_row_b
                else:
                    tac = r("scalar", lambda s_, sl=sl: s_.activation(
                        r_row[0:1, sl], r_row[0:1, sl], AF.Exp, scale=-0.5), inc="ac")
                    p.wait("vector", tac)
                    tdv = r("vector", lambda v, sl=sl: v.tensor_mul(
                        mr_row[0:1, sl], m_row[0:1, sl], r_row[0:1, sl]), inc="dv")
                    r_bc, mr_bc, ones_bc = r_row[0:1, sl], mr_row[0:1, sl], ones_row
                p.wait("tensor", tac)
                p.wait("tensor", tdv)
                tmm = r("tensor", lambda t_, r_bc=r_bc, ones_bc=ones_bc: t_.matmul(
                    pbc[:], ones_bc[:], r_bc, start=True, stop=True), inc="pe")
                p.wait("vector", tmm)
                r("vector", lambda v, sl=sl: v.tensor_copy(rstd_b[:, sl], pbc[:]))
                if need_rstd1:
                    r("vector", lambda v, sl=sl: v.tensor_scalar_add(rstd1_b[0][:, sl], pbc[:], 1.0))
                    if tg == NG - 1:
                        r("vector", lambda v: v.tensor_copy(m_last[:], m_row[0:1, S - 1:S]))
                        r("vector", lambda v: v.tensor_copy(r_last[:], r_row[0:1, S - 1:S]))
                ta = p.retro("vector")
                p.wait("tensor", ta)
                tmm = r("tensor", lambda t_, mr_bc=mr_bc, ones_bc=ones_bc: t_.matmul(
                    pbc[:], ones_bc[:], mr_bc, start=True, stop=True), inc="pe")
                p.wait("vector", tmm)
                ta = r("vector", lambda v, sl=sl: v.tensor_copy(mrstd_b[:, sl], pbc[:]), inc="dv")
                p.wait("tensor", ta)
            if DEBUG_AA and DBG_STATE.get("lnd") is None:
                DBG_STATE["lnd"] = True
                p.barrier()
                for nm, t_row in (("d_mrow", m_row), ("d_rrow", r_row), ("d_vrow", v_row)):
                    p.dma("sync", lambda sy, nm=nm, t_row=t_row: sy.dma_start(dbg[nm][:], t_row[:]))
                p.barrier()
            p.barrier()

    def emit_xn(tg, dep=None):
        sl = slice(tg * 512, (tg + 1) * 512)
        p.wait("vector", dep)
        tk = None
        for c in range(NCH):
            r("vector", lambda v, c=c, sl=sl: v.tensor_mul(
                sq_scr[:], x_fm[:, c, sl], rstd_b[:, sl]))
            tk = r("vector", lambda v, c=c, sl=sl: v.tensor_sub(
                xn_sb[:, c, :], sq_scr[:], mrstd_b[:, sl]), inc="dv")
        return tk

    def emit_attn_a():
        emit_ln(need_rstd1=False)
        with ExitStack() as stk:
            wA = sb("wA", [128, NCH, D], BF16, stk)
            wB = sb("wB", [128, NCH, D], BF16, stk)
            ba = sb("ba", [1, 2 * D], BF16, stk)
            bkt = ba[0:1, 0:D]
            bvt = ba[0:1, D:2 * D]
            bqt = ba[0:1, 0:D]
            k_sb = sb("k_sb", [128, NCH, S], BF16, stk)
            v_sb = sb("v_sb", [128, NT, 16 * 65], BF16, stk)
            q_sb = sb("q_sb", [128, NCH, 512], BF16, stk)
            qk_tok = sb("qk_tok", [128, D], BF16, stk)
            qsw = sb("qsw", [128, D], BF16, stk)
            rop_a = sb("rop_a", [128, D], BF16, stk)
            e_sb = sb("e_sb", [128, 2, 1024], BF16, stk)
            afm = xn_sb
            rrow = sq_scr[0:1, :]
            rb_s = sq_scr[64:128, :]

            pqk = psum_("pqk", [128, 512], F32, stk)
            ptr = psum_("ptr", [128, 1024], BF16, stk)
            pstb = [psum_("pst", [128, 2, 512], F32, stk) for _ in range(2)]
            ppvb = [psum_("ppv", [128, 512], F32, stk) for _ in range(2)]

            wtok = {}

            def load_w(nm, dst, src):
                for hf in range(2):
                    wtok[(nm, hf)] = p.dma("gpsimd", lambda g, dst=dst, src=src, hf=hf: g.dma_start(
                        dst[:, hf * 4:hf * 4 + 4, :],
                        src.rearrange("(c p) j -> p c j", p=128)[:, hf * 4:hf * 4 + 4, :]))
            for tt in range(NT):
                r("gpsimd", lambda g, tt=tt: g.memset(
                    v_sb[:, tt, :].rearrange("p (h d) -> p h d", d=65)[:, :, 64:65],
                    1.0))
            vones_tok = p.retro("gpsimd")
            p.wait("tensor", vones_tok)
            load_w("wk", wA, dr["aWkT"])
            load_w("wv", wB, dr["aWvT"])
            wtok["bk"] = p.dma("gpsimd", lambda g: g.dma_start(bkt, dr["abk"][:]))
            wtok["bv"] = p.dma("gpsimd", lambda g: g.dma_start(bvt, dr["abv"][:]))
            if DEBUG_AA and not DBG_STATE["dumped"]:
                DBG_STATE["dumped"] = True
                p.wait("sync", wtok[("wk", 0)])
                p.wait("sync", wtok[("wk", 1)])
                p.dma("sync", lambda sy: sy.dma_start(dbg["d_wa"][:], wA[:]))
                p.barrier()

            def proj_half(wname, wsb_, bias_sb, tt, tg, hf, free_tok, bank=None):
                bank = pqk if bank is None else bank
                t0 = (tt - tg * 4) * 128
                p.wait("tensor", free_tok)
                for c in range(NCH):
                    if c == 0:
                        p.wait("tensor", wtok[(wname, 0)])
                    if c == 4:
                        p.wait("tensor", wtok[(wname, 1)])
                    r("tensor", lambda t_, c=c, hf=hf, t0=t0, bank=bank: t_.matmul(
                        bank[:], xn_sb[:, c, t0:t0 + 128],
                        wsb_[:, c, hf * 512:(hf + 1) * 512],
                        start=(c == 0), stop=(bias_sb is None and c == NCH - 1)))
                if bias_sb is not None:
                    p.wait("tensor", wtok["b" + wname[1]])
                    boff = {"wk": 0, "wv": D, "wq": 0}[wname]
                    r("tensor", lambda t_, hf=hf, boff=boff, bank=bank: t_.matmul(
                        bank[:], ones_row_b[:], ba[0:1, boff + hf * 512:boff + (hf + 1) * 512],
                        start=False, stop=True))
                return p.retro("tensor")

            def produce_tok(wname, wsb_, bias_sb, dst_tile, tt, tg, free_tok, guard):
                banks = (pqk, ppvb[0])
                tmm = [None, None]
                for hf in range(2):
                    tmm[hf] = proj_half(wname, wsb_, bias_sb, tt, tg, hf,
                                        free_tok, bank=banks[hf])
                tev = None
                for hf in range(2):
                    p.wait("scalar", tmm[hf])
                    if hf == 0:
                        p.wait("scalar", guard)
                    tev = r("scalar", lambda s_, hf=hf: s_.copy(
                        dst_tile[:, hf * 512:(hf + 1) * 512], banks[hf][:]), inc="ac")
                return tev

            def rope(tt, src_tok):
                p.wait("vector", src_tok)
                r("vector", lambda v: v.tensor_copy(qsw[:, 0:D:2], qk_tok[:, 1:D:2]))
                r("vector", lambda v: v.tensor_copy(qsw[:, 1:D:2], qk_tok[:, 0:D:2]))
                c3 = c2t[:, tt, :][:, None, :].broadcast_to([128, 16, 64])
                s3 = s2t[:, tt, :][:, None, :].broadcast_to([128, 16, 64])
                r("vector", lambda v, c3=c3: v.tensor_tensor(
                    rop_a[:].rearrange("p (h d) -> p h d", h=16),
                    qk_tok[:].rearrange("p (h d) -> p h d", h=16), c3, ALU.mult))
                r("vector", lambda v, s3=s3: v.tensor_tensor(
                    qsw[:].rearrange("p (h d) -> p h d", h=16),
                    qsw[:].rearrange("p (h d) -> p h d", h=16), s3, ALU.mult))
                return r("vector", lambda v: v.tensor_add(rop_a[:], rop_a[:], qsw[:]), inc="dv")

            def transpose_pairs(dst_ap, rd_tok):
                p.wait("tensor", rd_tok)
                for hp in range(NCH):
                    r("tensor", lambda t_, hp=hp: t_.transpose(
                        ptr[:, hp * 128:(hp + 1) * 128],
                        rop_a[:, hp * 128:(hp + 1) * 128], ident_b[:]))
                ttr = p.retro("tensor")
                p.wait("vector", ttr)
                return r("vector", lambda v, dst_ap=dst_ap: v.tensor_copy(
                    dst_ap, ptr[:].rearrange("p (c t) -> p c t", c=NCH)), inc="dv")

            free_q = None
            xn_pe = None
            rope_guard = None
            for tg in range(NG):
                p.wait("vector", xn_pe)
                txn = emit_xn(tg, dep=None)
                p.wait("tensor", txn)
                for tl in range(4):
                    tt = tg * 4 + tl
                    if "a1_tile" in EXTRA_BARRIERS:
                        p.barrier()
                    tev = produce_tok("wk", wA, bkt, qk_tok, tt, tg, free_q, rope_guard)
                    trp = rope(tt, tev)
                    rope_guard = trp
                    if DEBUG_AA and tt == 0:
                        p.barrier()
                        for nm, t_sb in (("d_xn", xn_sb), ("d_c2t", c2t), ("d_s2t", s2t),
                                         ("d_kraw", qk_tok), ("d_krop", rop_a)):
                            p.dma("sync", lambda sy, nm=nm, t_sb=t_sb: sy.dma_start(
                                dbg[nm][:], t_sb[:]))
                        p.barrier()
                    transpose_pairs(k_sb[:, :, tt * 128:(tt + 1) * 128], trp)
                    vbanks = (pqk, ppvb[0])
                    tmmv = [None, None]
                    for hf in range(2):
                        tmmv[hf] = proj_half("wv", wB, bvt, tt, tg, hf, tev,
                                             bank=vbanks[hf])
                    tevv = None
                    for hf in range(2):
                        p.wait("scalar", tmmv[hf])
                        tevv = r("scalar", lambda s_, hf=hf, tt=tt: s_.copy(
                            v_sb[:, tt, hf * 520:(hf + 1) * 520].rearrange(
                                "p (h d) -> p h d", d=65)[:, :, 0:64],
                            vbanks[hf][:].rearrange("p (h d) -> p h d", d=64)), inc="ac")
                    free_q = tevv
                xn_pe = p.tok("pe")
            p.wait("gpsimd", xn_pe)
            load_w("wq", wA, dr["aWqT"])
            load_w("wo", wB, dr["aWoT"])
            wtok["bq"] = p.dma("gpsimd", lambda g: g.dma_start(bqt, dr["abq"][:]))
            o_read_tok = None
            exp_tok = [None, None]
            radd_tok = [None, None]
            pv_done = None
            for qg in range(NG):
                p.wait("vector", xn_pe)
                p.wait("vector", o_read_tok)
                txn = emit_xn(qg, dep=None)
                p.wait("tensor", txn)
                tev_t = None
                for tl in range(4):
                    tt = qg * 4 + tl
                    tev = produce_tok("wq", wA, bqt, qk_tok, tt, qg,
                                      free_q if tl == 0 else tev_t, rope_guard)
                    trp = rope(tt, tev)
                    rope_guard = trp
                    p.wait("vector", o_read_tok)
                    tev_t = transpose_pairs(q_sb[:, :, tl * 128:(tl + 1) * 128], trp)
                xn_pe = p.tok("pe")
                free_q = tev_t
                if "postQ" in EXTRA_BARRIERS:
                    p.barrier()
                for hp in range(NCH):
                    if "hp" in EXTRA_BARRIERS:
                        p.barrier()
                    hA, hB = 2 * hp, 2 * hp + 1

                    def emit_pv(kcp, hp=hp, hA=hA, hB=hB):
                        bufp = kcp % 2
                        if kcp == 0:
                            p.wait("tensor", pv_done)
                        p.wait("tensor", exp_tok[bufp])
                        st_fl, sp_fl = (kcp == 0), (kcp == NT - 1)
                        for hh, hx in ((0, hA), (1, hB)):
                            r("tensor", lambda t_, kcp=kcp, hx=hx, hh=hh, bufp=bufp, st_fl=st_fl, sp_fl=sp_fl: t_.matmul(
                                ppvb[hh][0:65, :], v_sb[:, kcp, hx * 65:(hx + 1) * 65],
                                e_sb[:, bufp, hh * 512:(hh + 1) * 512],
                                start=st_fl, stop=sp_fl, skip_group_check=True))

                    for kc in range(NT):
                        buf = kc % 2
                        p.wait("tensor", exp_tok[buf])
                        p.wait("tensor", tev_t)
                        r("tensor", lambda t_, hp=hp, kc=kc, buf=buf: t_.matmul(
                            pstb[buf][:, 0, :],
                            k_sb[0:64, hp, kc * 128:(kc + 1) * 128],
                            q_sb[0:64, hp, :], start=True, stop=True))
                        tst = r("tensor", lambda t_, hp=hp, kc=kc, buf=buf: t_.matmul(
                            pstb[buf][:, 1, :],
                            k_sb[64:128, hp, kc * 128:(kc + 1) * 128],
                            q_sb[64:128, hp, :], start=True, stop=True), inc="pe")
                        p.wait("scalar", tst)
                        exp_tok[buf] = r("scalar", lambda s_, buf=buf: s_.activation(
                            e_sb[:, buf, :], pstb[buf][:, :, :],
                            AF.Exp, scale=INV_SCALE_A), inc="ac")
                        if kc > 0:
                            emit_pv(kc - 1)
                    emit_pv(NT - 1)
                    tok_pv = p.retro("tensor")
                    prev_bc = None
                    nrm_tok = None
                    for hh in range(2):
                        pbase = hh * 64
                        p.wait("scalar", tok_pv)
                        p.wait("scalar", prev_bc)
                        r("scalar", lambda s_, hh=hh: s_.activation(
                            rrow, ppvb[hh][64:65, :], AF.Ln))
                        trc = r("scalar", lambda s_: s_.activation(
                            rrow, rrow, AF.Exp, scale=-1.0), inc="ac")
                        p.wait("tensor", trc)
                        tbc = r("tensor", lambda t_, hh=hh: t_.matmul(
                            ppvb[hh][64:128, :], ones_64[:], rrow,
                            start=True, stop=True, skip_group_check=True,
                            tile_position=(0, 64)), inc="pe")
                        prev_bc = tbc
                        p.wait("scalar", tbc)
                        tevb = r("scalar", lambda s_, hh=hh: s_.copy(
                            rb_s, ppvb[hh][64:128, :]), inc="ac")
                        p.wait("vector", tevb)
                        p.wait("vector", o_read_tok)
                        nrm_tok = r("vector", lambda v, hh=hh, pbase=pbase, hp=hp: v.tensor_mul(
                            afm[pbase:pbase + 64, hp, :], ppvb[hh][0:64, :],
                            rb_s), inc="dv")
                        p.wait("scalar", nrm_tok)
                    pv_done = nrm_tok
                if DEBUG_AA and qg == 0:
                    p.barrier()
                    for nm, t_sb in (("d_k", k_sb), ("d_v", v_sb), ("d_q", q_sb),
                                     ("d_afm", afm), ("d_e", e_sb), ("d_x", x_fm),
                                     ("d_rstd", rstd_b), ("d_mr", mrstd_b)):
                        p.dma("sync", lambda sy, nm=nm, t_sb=t_sb: sy.dma_start(
                            dbg[nm][:], t_sb[:]))
                    p.barrier()
                if "preO" in EXTRA_BARRIERS:
                    p.barrier()
                for jc in range(NCH):
                    buf = jc % 2
                    p.wait("tensor", radd_tok[buf])
                    p.wait("tensor", pv_done)
                    p.wait("tensor", exp_tok[buf])
                    for c in range(NCH):
                        if jc == 0 and c == 0:
                            p.wait("tensor", wtok[("wo", 0)])
                        if jc == 0 and c == 4:
                            p.wait("tensor", wtok[("wo", 1)])
                        r("tensor", lambda t_, c=c, jc=jc, buf=buf: t_.matmul(
                            pstb[buf][:, 0, :], wB[:, c, jc * 128:(jc + 1) * 128],
                            afm[:, c, :], start=(c == 0), stop=(c == NCH - 1)))
                    tok_o = p.retro("tensor")
                    o_read_tok = tok_o
                    p.wait("vector", tok_o)
                    sl = slice(qg * 512, (qg + 1) * 512)
                    radd_tok[buf] = r("vector", lambda v, jc=jc, buf=buf, sl=sl: v.tensor_add(
                        x_fm[:, jc, sl], x_fm[:, jc, sl], pstb[buf][:, 0, :]), inc="dv")
                exp_tok[0] = None
                exp_tok[1] = None
                pv_done = radd_tok[1] or radd_tok[0]
            p.barrier()

    def emit_ffn(stream_out=None):
        with ExitStack() as stk:
            rstd1_b[0] = sb("rstd1_b", [128, S], F32, stk)
            emit_ln(need_rstd1=True)
            w1s = sb("w1s", [128, NCH, FF], BF16, stk)
            w2s = sb("w2s", [128, NFC, D], BF16, stk)
            wof = sb("wof", [128, NCH, D], BF16, stk)
            wp1 = sb("wp1", [128, NCH, NAV], F32, stk)
            wp2 = sb("wp2", [128, NEXP], F32, stk)
            bp1 = sb("bp1", [1, NAV], F32, stk)
            bp2 = sb("bp2", [1, NEXP], F32, stk)
            b1f = sb("b1f", [128, NFC], F32, stk)
            b2f = sb("b2f", [128, NCH], F32, stk)
            bo3 = sb("bo3", [128, NCH], F32, stk)
            xl = sb("xl", [128, NCH], F32, stk)
            mb128 = sb("mb128", [128, 1], F32, stk)
            rb128 = sb("rb128", [128, 1], F32, stk)
            grow = sb("grow", [1, NAV], F32, stk)
            gcol = sb("gcol", [128, 1], F32, stk)
            lg = sb("lg", [1, NEXP], F32, stk)
            mx1 = sb("mx1", [1, 1], F32, stk)
            oh8 = sb("oh8", [1, NEXP], F32, stk)
            self_f = sb("self_f", [1, 1], F32, stk)
            base1 = sb("base1", [128, 1], F32, stk)
            idxw1 = sb("idxw1", [128, NCH], I32, stk)
            idxw2 = sb("idxw2", [128, NFC], I32, stk)
            idxb = sb("idxb", [128, 1], I32, stk)
            idxf_s = sb("idxf_s", [128, 1], F32, stk)
            h1_sb = sb("h1_sb", [128, NFC, 512], BF16, stk)
            rs_sb = xn_sb
            o3_sb = sb("o3_sb", [128, 512], F32, stk)
            xn32 = sb("xn32", [128, 512], F32, stk)

            with ExitStack() as rtk:
                prt_t = psum_("prt", [1, 512], F32, rtk)
                prt = prt_t[0:1, 0:128]
                prt8_t = psum_("prt8", [1, 512], F32, rtk)
                prt8 = prt8_t[0:1, 0:NEXP]
                pgt_t = psum_("pgt", [128, 512], F32, rtk)
                pgt = pgt_t[:, 0:1]
                pbs_t = psum_("pbs", [128, 512], F32, rtk)
                pbs = pbs_t[:, 0:1]

                t_wp1 = p.dma("sync", lambda sy: sy.dma_start(
                    wp1[:], dr["fWp1"].rearrange("(c p) f -> p c f", p=128)))
                t_wp2 = p.dma("sync", lambda sy: sy.dma_start(wp2[:], dr["fWp2"][:]))
                t_bp1 = p.dma("sync", lambda sy: sy.dma_start(bp1[:], dr["fbp1"][:]))
                t_bp2 = p.dma("sync", lambda sy: sy.dma_start(bp2[:], dr["fbp2"][:]))
                t_bo = p.dma("sync", lambda sy: sy.dma_start(bo3[:], dr["fbo_fm"][:]))
                twof = [p.dma("gpsimd", lambda g, hf=hf: g.dma_start(
                    wof[:, hf * 4:hf * 4 + 4, :],
                    dr["fWo"].rearrange("(c p) j -> p c j", p=128)[:, hf * 4:hf * 4 + 4, :]))
                    for hf in range(2)]

                tml = r("tensor", lambda t_: t_.matmul(
                    pgt, ones_row[:], m_last[:], start=True, stop=True))
                tml = r("tensor", lambda t_: t_.matmul(
                    pbs[:], ones_row[:], r_last[:], start=True, stop=True), inc="pe")
                p.wait("scalar", tml)
                r("scalar", lambda s_: s_.copy(mb128[:], pgt))
                tml2 = r("scalar", lambda s_: s_.copy(rb128[:], pbs), inc="ac")
                p.wait("vector", tml2)
                for c in range(NCH):
                    r("vector", lambda v, c=c: v.tensor_scalar(
                        xl[:, c:c + 1], x_fm[:, c, S - 1:S], mb128[:], rb128[:],
                        ALU.subtract, ALU.mult))
                txl = p.retro("vector")
                p.wait("tensor", txl)
                p.wait("tensor", t_wp1)
                for c in range(NCH):
                    r("tensor", lambda t_, c=c: t_.matmul(
                        prt, xl[:, c:c + 1], wp1[:, c, :],
                        start=(c == 0), stop=(c == NCH - 1)))
                tmm = p.retro("tensor")
                p.wait("vector", tmm)
                p.wait("vector", t_bp1)
                tdv = r("vector", lambda v: v.tensor_add(grow[:], prt, bp1[:]), inc="dv")
                p.wait("scalar", tdv)
                tac = r("scalar", lambda s_: s_.activation(grow[:], grow[:], AF.Gelu), inc="ac")
                p.wait("tensor", tac)
                tmm = r("tensor", lambda t_: t_.matmul(
                    pgt, grow[:], ones_1_f[:], start=True, stop=True), inc="pe")
                p.wait("scalar", tmm)
                tac = r("scalar", lambda s_: s_.copy(gcol[:], pgt), inc="ac")
                p.wait("tensor", tac)
                p.wait("tensor", t_wp2)
                tmm = r("tensor", lambda t_: t_.matmul(
                    prt8, gcol[:], wp2[:], start=True, stop=True), inc="pe")
                p.wait("vector", tmm)
                p.wait("vector", t_bp2)
                tdv = r("vector", lambda v: v.tensor_add(lg[:], prt8, bp2[:]), inc="dv")
                tdv = r("vector", lambda v: v.reduce_max(mx1[:], lg[:], axis=AX.X), inc="dv")
                p.wait("vector", tdv)
                r("vector", lambda v: v.tensor_scalar(oh8[:], lg[:], mx1[:], None, ALU.is_ge))
                r("vector", lambda v: v.tensor_mul(oh8[:], oh8[:], rev8[:]))
                tdv = r("vector", lambda v: v.reduce_max(self_f[:], oh8[:], axis=AX.X), inc="dv")
                p.wait("vector", tdv)
                tsel = r("vector", lambda v: v.tensor_scalar(
                    self_f[:], self_f[:], -1.0, float(NEXP), ALU.mult, ALU.add), inc="dv")
                p.wait("tensor", tsel)
                tmm = r("tensor", lambda t_: t_.matmul(
                    pbs, ones_row[:], self_f[:], start=True, stop=True), inc="pe")
                p.wait("vector", tmm)
                r("vector", lambda v: v.scalar_tensor_tensor(
                    base1[:], pbs, float(D), iota_p[:], ALU.mult, ALU.add))
                for c in range(NCH):
                    r("vector", lambda v, c=c: v.tensor_scalar_add(
                        idxf_s[:], base1[:], float(c * 128)))
                    r("vector", lambda v, c=c: v.tensor_copy(idxw1[:, c:c + 1], idxf_s[:]))
                r("vector", lambda v: v.scalar_tensor_tensor(
                    base1[:], pbs, float(FF), iota_p[:], ALU.mult, ALU.add))
                for c in range(NFC):
                    r("vector", lambda v, c=c: v.tensor_scalar_add(
                        idxf_s[:], base1[:], float(c * 128)))
                    r("vector", lambda v, c=c: v.tensor_copy(idxw2[:, c:c + 1], idxf_s[:]))
                r("vector", lambda v: v.scalar_tensor_tensor(
                    base1[:], pbs, 128.0, iota_p[:], ALU.mult, ALU.add))
                tidx = r("vector", lambda v: v.tensor_copy(idxb[:], base1[:]), inc="dv")
                p.wait("gpsimd", tidx)
                gtok_w1 = [p.dma("gpsimd", lambda g, c=c: g.indirect_dma_start(
                    w1s[:, c, :], None, dr["fW1"][:],
                    IndirectOffsetOnAxis(idxw1[:, c:c + 1], 0))) for c in range(NCH)]
                gtok_w2 = [p.dma("gpsimd", lambda g, c=c: g.indirect_dma_start(
                    w2s[:, c, :], None, dr["fW2"][:],
                    IndirectOffsetOnAxis(idxw2[:, c:c + 1], 0))) for c in range(NFC)]
                g_b1 = p.dma("gpsimd", lambda g: g.indirect_dma_start(
                    b1f[:], None, dr["fb1r"][:], IndirectOffsetOnAxis(idxb[:], 0)))
                g_b2 = p.dma("gpsimd", lambda g: g.indirect_dma_start(
                    b2f[:], None, dr["fb2r"][:], IndirectOffsetOnAxis(idxb[:], 0)))
                p.wait("vector", t_bo)
                tbo3 = r("vector", lambda v: v.tensor_scalar_mul(bo3[:], bo3[:], 0.3), inc="dv")

            with ExitStack() as mtk:
                phb = [psum_("ph", [128, 512], F32, mtk) for _ in range(2)]
                prb = [psum_("pr", [128, 512], F32, mtk) for _ in range(2)]
                pob = [psum_("po", [128, 512], F32, mtk) for _ in range(2)]

                xn_dep = None
                xn_pe = None
                h_free = [None, None]
                r_free = [None, None]
                o_free = [None, None]
                h_rd = None
                rs_rd = None
                for tg in range(NG):
                    sl = slice(tg * 512, (tg + 1) * 512)
                    p.wait("vector", xn_pe)
                    txn = emit_xn(tg, dep=None)
                    p.wait("tensor", txn)
                    gel_tok = None
                    for fc in range(NFC):
                        buf = fc % 2
                        p.wait("tensor", h_free[buf])
                        for c in range(NCH):
                            if fc == 0:
                                p.wait("tensor", gtok_w1[c])
                            r("tensor", lambda t_, c=c, fc=fc, buf=buf: t_.matmul(
                                phb[buf][:], w1s[:, c, fc * 128:(fc + 1) * 128],
                                xn_sb[:, c, :], start=(c == 0), stop=(c == NCH - 1)))
                        th = p.retro("tensor")
                        p.wait("scalar", th)
                        if fc == 0:
                            p.wait("scalar", g_b1)
                        p.wait("scalar", h_rd)
                        gel_tok = r("scalar", lambda s_, fc=fc, buf=buf: s_.activation(
                            h1_sb[:, fc, :], phb[buf][:], AF.Gelu,
                            bias=b1f[:, fc:fc + 1]), inc="ac")
                        h_free[buf] = gel_tok
                    xn_pe = p.tok("pe")
                    ev_tok = None
                    for jc in range(NCH):
                        buf = jc % 2
                        p.wait("tensor", r_free[buf])
                        p.wait("tensor", gel_tok)
                        for fc in range(NFC):
                            if jc == 0:
                                p.wait("tensor", gtok_w2[fc])
                            r("tensor", lambda t_, fc=fc, jc=jc, buf=buf: t_.matmul(
                                prb[buf][:], w2s[:, fc, jc * 128:(jc + 1) * 128],
                                h1_sb[:, fc, :], start=(fc == 0), stop=(fc == NFC - 1)))
                        tr_ = p.retro("tensor")
                        h_rd = tr_
                        p.wait("scalar", tr_)
                        if jc == 0:
                            p.wait("scalar", g_b2)
                        p.wait("scalar", rs_rd)
                        ev_tok = r("scalar", lambda s_, jc=jc, buf=buf: s_.activation(
                            rs_sb[:, jc, :], prb[buf][:], AF.Identity,
                            bias=b2f[:, jc:jc + 1]), inc="ac")
                        r_free[buf] = ev_tok
                    if DEBUG_FF and tg == 0 and not DBG_STATE.get("ffd"):
                        DBG_STATE["ffd"] = True
                        p.barrier()
                        for nm, t_sb in (("f_lg", lg), ("f_sel", self_f), ("f_xl", xl),
                                         ("f_g", grow), ("f_w1", w1s), ("f_w2", w2s),
                                         ("f_b1", b1f), ("f_h1", h1_sb), ("f_rs", rs_sb),
                                         ("f_idx", idxw1)):
                            p.dma("sync", lambda sy, nm=nm, t_sb=t_sb: sy.dma_start(
                                dbgf[nm][:], t_sb[:]))
                        p.barrier()
                    for jc2 in range(NCH):
                        buf = jc2 % 2
                        p.wait("tensor", o_free[buf])
                        p.wait("tensor", ev_tok)
                        for jc in range(NCH):
                            if jc2 == 0 and jc == 0:
                                p.wait("tensor", twof[0])
                            if jc2 == 0 and jc == 4:
                                p.wait("tensor", twof[1])
                            r("tensor", lambda t_, jc=jc, jc2=jc2, buf=buf: t_.matmul(
                                pob[buf][:], wof[:, jc, jc2 * 128:(jc2 + 1) * 128],
                                rs_sb[:, jc, :], start=(jc == 0), stop=(jc == NCH - 1)))
                        to_ = p.retro("tensor")
                        rs_rd = to_
                        p.wait("scalar", to_)
                        p.wait("scalar", tbo3)
                        tev = r("scalar", lambda s_, jc2=jc2, buf=buf: s_.activation(
                            o3_sb[:], pob[buf][:], AF.Identity,
                            bias=bo3[:, jc2:jc2 + 1], scale=0.3), inc="ac")
                        o_free[buf] = tev
                        p.wait("vector", tev)
                        r("vector", lambda v, jc2=jc2, sl=sl: v.tensor_mul(
                            xn32[:], x_fm[:, jc2, sl], rstd1_b[0][:, sl]))
                        r("vector", lambda v, jc2=jc2, sl=sl: v.tensor_sub(
                            xn32[:], xn32[:], mrstd_b[:, sl]))
                        tra = r("vector", lambda v, jc2=jc2, sl=sl: v.tensor_add(
                            x_fm[:, jc2, sl], xn32[:], o3_sb[:]), inc="dv")
                        p.wait("scalar", tra)
                    if stream_out is not None:
                        p.wait("sync", tra)
                        p.dma("sync", lambda sy, sl=sl: sy.dma_start(
                            stream_out[:, :, sl], x_fm[:, :, sl]))
                    xn_pe = rs_rd
                p.barrier()

    def emit_attn_b():
        emit_ln(need_rstd1=False)
        with ExitStack() as stk:
            wq = sb("wq", [128, NCH, D], BF16, stk)
            wk = sb("wk", [128, NCH, D], BF16, stk)
            wv = sb("wv", [128, NCH, D], BF16, stk)
            wo = sb("wo", [128, NCH, D], BF16, stk)
            bo_fm = sb("bo_fm", [128, NCH], F32, stk)
            q_tok = sb("q_tok", [128, D], BF16, stk)
            qs_tok = sb("qs_tok", [128, D], BF16, stk)
            mx16 = sb("mx16", [128, 16], F32, stk)
            sm16 = sb("sm16", [128, 16], F32, stk)
            kp_sb = sb("kp_sb", [128, 4, D], BF16, stk)
            vb_sb = sb("vb_sb", [128, 4, D], BF16, stk)
            qb_sb = sb("qb_sb", [128, NCH, 512], BF16, stk)
            cn_sb = sb("cn_sb", [128, NCH, 64], BF16, stk)
            csum = sb("csum", [128, NCH, 64], F32, stk)
            kssum = sb("kssum", [128, NCH], F32, stk)
            rks = sb("rks", [128, NCH], F32, stk)
            abfm = sb("abfm", [128, NCH, 512], BF16, stk)
            o3b = sb("o3b", [128, 512], F32, stk)
            t0_s = sb("t0_s", [128, D], BF16, stk)
            t1_s = sb("t1_s", [128, D], BF16, stk)

            pqk = psum_("pqkb", [128, 512], F32, stk)
            ptr = psum_("ptrb", [128, 1024], BF16, stk)
            pC = psum_("pC", [128, NCH, 64], F32, stk)
            pks_t = psum_("pks", [128, 512], F32, stk)
            pks = pks_t
            pao = psum_("pao", [128, NCH, 64], F32, stk)
            pob2 = [psum_("pob", [128, 512], F32, stk) for _ in range(2)]

            wtok = {}
            for nm, dst, src in (("wq", wq, dr["bWqT"]), ("wk", wk, dr["bWkT"]),
                                 ("wv", wv, dr["bWvT"]), ("wo", wo, dr["bWoT"])):
                for hf in range(2):
                    wtok[(nm, hf)] = p.dma("gpsimd", lambda g, dst=dst, src=src, hf=hf: g.dma_start(
                        dst[:, hf * 4:hf * 4 + 4, :],
                        src.rearrange("(c p) j -> p c j", p=128)[:, hf * 4:hf * 4 + 4, :]))
            t_bo = p.dma("gpsimd", lambda g: g.dma_start(bo_fm[:], dr["bbo_fm"][:]))

            def proj_half(wname, wsb_, tt, tg, hf, free_tok):
                t0 = (tt - tg * 4) * 128
                p.wait("tensor", free_tok)
                for c in range(NCH):
                    if c == 0:
                        p.wait("tensor", wtok[(wname, 0)])
                    if c == 4:
                        p.wait("tensor", wtok[(wname, 1)])
                    r("tensor", lambda t_, c=c, hf=hf, t0=t0: t_.matmul(
                        pqk[:], xn_sb[:, c, t0:t0 + 128],
                        wsb_[:, c, hf * 512:(hf + 1) * 512],
                        start=(c == 0), stop=(c == NCH - 1)))
                return p.retro("tensor")

            xn_pe = None
            free_q = None
            buck_tok = None
            tao_prev = None
            pao_ev = None
            o_read = None
            tkp_prev = None
            q_guard = None
            radd = [None, None]
            for tg in range(NG):
                sl = slice(tg * 512, (tg + 1) * 512)
                p.wait("vector", xn_pe)
                txn = emit_xn(tg, dep=None)
                p.wait("tensor", txn)
                for tl in range(4):
                    if "qkv" in SKIP_AB:
                        break
                    tt = tg * 4 + tl
                    tev = None
                    for hf in range(2):
                        tmm = proj_half("wq", wq, tt, tg, hf, free_q if hf == 0 else tev)
                        p.wait("scalar", tmm)
                        if hf == 0:
                            p.wait("scalar", q_guard)
                        tev = r("scalar", lambda s_, hf=hf: s_.copy(
                            q_tok[:, hf * 512:(hf + 1) * 512], pqk[:]), inc="ac")
                    tevk = None
                    for hf in range(2):
                        tmm = proj_half("wk", wk, tt, tg, hf, tev if hf == 0 else tevk)
                        p.wait("scalar", tmm)
                        p.wait("scalar", tkp_prev)
                        tevk = r("scalar", lambda s_, hf=hf: s_.activation(
                            t0_s[:, hf * 512:(hf + 1) * 512], pqk[:],
                            AF.Identity, scale=SCALE_B), inc="ac")
                    tevv = None
                    for hf in range(2):
                        tmm = proj_half("wv", wv, tt, tg, hf, tevk if hf == 0 else tevv)
                        p.wait("vector", tmm)
                        tevv = r("vector", lambda v, hf=hf, tl=tl: v.tensor_copy(
                            vb_sb[:, tl, hf * 512:(hf + 1) * 512], pqk[:]), inc="dv")
                    free_q = tevv
                    if "softmax" in SKIP_AB:
                        p.wait("vector", tev)
                        tqn = r("vector", lambda v: v.tensor_copy(qs_tok[:], q_tok[:]), inc="dv")
                    q3 = q_tok[:].rearrange("p (h d) -> p h d", h=16)
                    p.wait("vector", tev)
                    tm = None
                    if "softmax" not in SKIP_AB:
                        tm = r("vector", lambda v, q3=q3: v.reduce_max(
                        mx16[:], q3, axis=AX.X), inc="dv")
                    if "softmax" not in SKIP_AB:
                        p.wait("vector", tm)
                        tsb_ = r("vector", lambda v, q3=q3: v.tensor_tensor(
                            qs_tok[:].rearrange("p (h d) -> p h d", h=16), q3,
                            mx16[:, :, None].broadcast_to([128, 16, 64]),
                            ALU.subtract), inc="dv")
                        q_guard = tsb_
                        p.wait("scalar", tsb_)
                        te = r("scalar", lambda s_: s_.activation(
                            qs_tok[:], qs_tok[:], AF.Exp, scale=SCALE_B), inc="ac")
                        p.wait("vector", te)
                        ts = r("vector", lambda v: v.reduce_sum(
                            sm16[:], qs_tok[:].rearrange("p (h d) -> p h d", h=16),
                            axis=AX.X), inc="dv")
                        p.wait("vector", ts)
                        ts = r("vector", lambda v: v.reciprocal(sm16[:], sm16[:]), inc="dv")
                        p.wait("vector", ts)
                        tqn = r("vector", lambda v: v.tensor_tensor(
                            qs_tok[:].rearrange("p (h d) -> p h d", h=16),
                            qs_tok[:].rearrange("p (h d) -> p h d", h=16),
                            sm16[:, :, None].broadcast_to([128, 16, 64]),
                            ALU.mult), inc="dv")
                        q_guard = tqn
                    p.wait("tensor", tqn)
                    for hp in range(NCH):
                        r("tensor", lambda t_, hp=hp: t_.transpose(
                            ptr[:, hp * 128:(hp + 1) * 128],
                            qs_tok[:, hp * 128:(hp + 1) * 128], ident_b[:]))
                    ttr = p.retro("tensor")
                    p.wait("vector", ttr)
                    p.wait("vector", o_read)
                    r("vector", lambda v, tl=tl: v.tensor_copy(
                        qb_sb[:, :, tl * 128:(tl + 1) * 128],
                        ptr[:].rearrange("p (c t) -> p c t", c=NCH)))
                    r("vector", lambda v: v.tensor_scalar_min(t1_s[:], t0_s[:], 0.0))
                    tmin = r("vector", lambda v: v.tensor_scalar_max(t0_s[:], t0_s[:], 0.0), inc="dv")
                    p.wait("scalar", tmin)
                    texp = r("scalar", lambda s_: s_.activation(t1_s[:], t1_s[:], AF.Exp), inc="ac")
                    p.wait("vector", texp)
                    tkp = r("vector", lambda v, tl=tl: v.tensor_add(
                        kp_sb[:, tl, :], t0_s[:], t1_s[:]), inc="dv")
                    tkp_prev = tkp
                    free_q = tkp
                xn_pe = p.tok("pe")
                for ul in range(8):
                    if "buckets" in SKIP_AB:
                        p.wait("vector", o_read)
                        pao_ev = r("vector", lambda v, ul=ul: v.memset(
                            abfm[:, :, ul * 64:(ul + 1) * 64], 0.125), inc="dv")
                        buck_tok = pao_ev
                        continue
                    u = tg * 8 + ul
                    tl = ul // 2
                    base = (ul % 2) * 64
                    if u > 0:
                        p.wait("vector", tao_prev)
                        r("vector", lambda v: v.tensor_scalar_add(rks[:], kssum[:], EPS_B))
                        trk = r("vector", lambda v: v.reciprocal(rks[:], rks[:]), inc="dv")
                        tcn = r("vector", lambda v: v.tensor_tensor(
                            cn_sb[:], csum[:],
                            rks[:, :, None].broadcast_to([128, NCH, 64]),
                            ALU.mult), inc="dv")
                    p.wait("tensor", buck_tok)
                    p.wait("tensor", free_q)
                    for hh in range(H):
                        cb_ = (hh % 2) * 64
                        r("tensor", lambda t_, hh=hh, cb_=cb_, base=base, tl=tl: t_.matmul(
                            pC[cb_:cb_ + 64, hh // 2, :],
                            kp_sb[base:base + 64, tl, hh * 64:(hh + 1) * 64],
                            vb_sb[base:base + 64, tl, hh * 64:(hh + 1) * 64],
                            start=(hh < 2), stop=(hh >= H - 2),
                            skip_group_check=True,
                            tile_position=(base, cb_)))
                    for hh in range(H):
                        cb_ = (hh % 2) * 64
                        r("tensor", lambda t_, hh=hh, cb_=cb_, base=base, tl=tl: t_.matmul(
                            pks[cb_:cb_ + 64, hh // 2:hh // 2 + 1],
                            kp_sb[base:base + 64, tl, hh * 64:(hh + 1) * 64],
                            ones_col_b[base:base + 64, :],
                            start=(hh < 2), stop=(hh >= H - 2),
                            skip_group_check=True,
                            tile_position=(base, cb_)))
                    tmm = p.retro("tensor")
                    if u > 0:
                        p.wait("tensor", tcn)
                        p.wait("tensor", pao_ev)
                        for hh in range(H):
                            cb_ = (hh % 2) * 64
                            r("tensor", lambda t_, hh=hh, cb_=cb_, ul=ul: t_.matmul(
                                pao[cb_:cb_ + 64, hh // 2, :],
                                cn_sb[cb_:cb_ + 64, hh // 2, :],
                                qb_sb[cb_:cb_ + 64, hh // 2, ul * 64:(ul + 1) * 64],
                                start=(hh < 2), stop=(hh >= H - 2),
                                skip_group_check=True,
                                tile_position=(cb_, cb_)))
                        tao = p.retro("tensor")
                        tao_prev = tao
                        p.wait("vector", tao)
                        pao_ev = r("vector", lambda v, ul=ul: v.tensor_copy(
                            abfm[:, :, ul * 64:(ul + 1) * 64], pao[:]), inc="dv")
                    else:
                        p.wait("vector", o_read)
                        pao_ev = r("vector", lambda v, ul=ul: v.memset(
                            abfm[:, :, ul * 64:(ul + 1) * 64], 0.0), inc="dv")
                    p.wait("vector", tmm)
                    if u == 0:
                        r("vector", lambda v: v.tensor_copy(csum[:], pC[:]))
                        buck_tok = r("vector", lambda v: v.tensor_copy(kssum[:], pks[:, 0:NCH]), inc="dv")
                    else:
                        r("vector", lambda v: v.tensor_add(csum[:], csum[:], pC[:]))
                        buck_tok = r("vector", lambda v: v.tensor_add(kssum[:], kssum[:], pks[:, 0:NCH]), inc="dv")
                for jc2 in range(NCH):
                    if "oproj" in SKIP_AB:
                        break
                    buf = jc2 % 2
                    p.wait("tensor", radd[buf])
                    p.wait("tensor", pao_ev)
                    for c in range(NCH):
                        if tg == 0 and jc2 == 0 and c == 0:
                            p.wait("tensor", wtok[("wo", 0)])
                        if tg == 0 and jc2 == 0 and c == 4:
                            p.wait("tensor", wtok[("wo", 1)])
                        r("tensor", lambda t_, c=c, jc2=jc2, buf=buf: t_.matmul(
                            pob2[buf][:], wo[:, c, jc2 * 128:(jc2 + 1) * 128],
                            abfm[:, c, :], start=(c == 0), stop=(c == NCH - 1)))
                    to_ = p.retro("tensor")
                    o_read = to_
                    if "oev" in SKIP_AB:
                        radd[buf] = to_
                        continue
                    p.wait("scalar", to_)
                    p.wait("scalar", t_bo)
                    tev = r("scalar", lambda s_, jc2=jc2, buf=buf: s_.activation(
                        o3b[:], pob2[buf][:], AF.Identity,
                        bias=bo_fm[:, jc2:jc2 + 1]), inc="ac")
                    if "oradd" in SKIP_AB:
                        radd[buf] = tev
                        continue
                    p.wait("vector", tev)
                    radd[buf] = r("vector", lambda v, jc2=jc2, sl=sl: v.tensor_add(
                        x_fm[:, jc2, sl], x_fm[:, jc2, sl], o3b[:]), inc="dv")
                    p.wait("scalar", radd[buf])
            p.barrier()

    ones_1_f = sb("ones_1f", [1, 1])

    r("gpsimd", lambda g: g.memset(ones_1_f[:], 1.0))
    emit_init()
    for seq in range(nseq):
        p.dma("sync", lambda sy, seq=seq: sy.dma_start(
            x_fm[:], xin[seq].rearrange("(c p) t -> p c t", p=128)))
        p.barrier()
        yout_fm = yout[seq].rearrange("(c p) t -> p c t", p=128)
        for i, phase in enumerate(phases):
            last = i == len(phases) - 1
            if phase == "attn_a":
                emit_attn_a()
            elif phase == "ffn":
                emit_ffn(stream_out=yout_fm if last else None)
            elif phase == "attn_b":
                emit_attn_b()
        if phases[-1] != "ffn":
            p.dma("sync", lambda sy, seq=seq: sy.dma_start(
                yout[seq].rearrange("(c p) t -> p c t", p=128), x_fm[:]))
        p.barrier()

    with nc.Block() as block:
        @block.tensor
        def _(e):
            for fn in p.streams["tensor"]:
                fn(e)

        @block.vector
        def _(e):
            for fn in p.streams["vector"]:
                fn(e)

        @block.scalar
        def _(e):
            for fn in p.streams["scalar"]:
                fn(e)

        @block.gpsimd
        def _(e):
            for fn in p.streams["gpsimd"]:
                fn(e)

        @block.sync
        def _(e):
            for fn in p.streams["sync"]:
                fn(e)
    es.close()
    return nc


def make_inputs(core, nseq, **w):
    xt = np.ascontiguousarray(
        w["L0_attn_a_x"][core * nseq:(core + 1) * nseq].transpose(0, 2, 1))

    def T(a):
        return np.ascontiguousarray(a.T)
    return {
        "xin": xt.astype(np.float32),
        "aWqT": T(w["L0_attn_a_Wq"]), "aWkT": T(w["L0_attn_a_Wk"]),
        "aWvT": T(w["L0_attn_a_Wv"]), "aWoT": T(w["L0_attn_a_Wo"]),
        "abq": w["L0_attn_a_bq"][None, :], "abk": w["L0_attn_a_bk"][None, :],
        "abv": w["L0_attn_a_bv"][None, :],
        "bWqT": T(w["L0_attn_b_Wq"]), "bWkT": T(w["L0_attn_b_Wk"]),
        "bWvT": T(w["L0_attn_b_Wv"]), "bWoT": T(w["L0_attn_b_Wo"]),
        "bbo_fm": np.ascontiguousarray(w["L0_attn_b_bo"].reshape(NCH, 128).T),
        "fWp1": w["L0_ffn_Wp1"], "fbp1": w["L0_ffn_bp1"][None, :],
        "fWp2": w["L0_ffn_Wp2"], "fbp2": w["L0_ffn_bp2"][None, :],
        "fW1": w["L0_ffn_W1"].reshape(NEXP * D, FF),
        "fW2": w["L0_ffn_W2"].reshape(NEXP * FF, D),
        "fb1r": np.ascontiguousarray(
            w["L0_ffn_b1"].reshape(NEXP, NFC, 128).transpose(0, 2, 1)).reshape(NEXP * 128, NFC),
        "fb2r": np.ascontiguousarray(
            w["L0_ffn_b2"].reshape(NEXP, NCH, 128).transpose(0, 2, 1)).reshape(NEXP * 128, NCH),
        "fWo": w["L0_ffn_Wo"],
        "fbo_fm": np.ascontiguousarray(w["L0_ffn_bo"].reshape(NCH, 128).T),
    }


def kernel(**inputs):
    inputs = {k: np.asarray(v, dtype=np.float32) for k, v in inputs.items()}
    ncores = 8
    nc = build_program()
    in_maps = [make_inputs(core, NSEQ, **inputs) for core in range(ncores)]
    res = run_bass_kernel_spmd(nc, in_maps, core_ids=list(range(ncores)))
    out = np.empty((B, S, D), dtype=np.float32)
    for core in range(ncores):
        y = np.asarray(res.results[core]["yout"])
        out[core * NSEQ:(core + 1) * NSEQ] = y.transpose(0, 2, 1)
    return out
